# version 02 (final)
# baseline (speedup 1.0000x reference)
import contextlib
import numpy as np
import concourse.bass as bass
import concourse.mybir as mybir
from concourse.bass_utils import run_bass_kernel_spmd

F32 = mybir.dt.float32
BF16 = mybir.dt.bfloat16
I32 = mybir.dt.int32
ALU = mybir.AluOpType
ACT = mybir.ActivationFunctionType

NC8 = 8
T = 8192
DM = 2048
DI = 4096
DIC = 512
NST = 16
DTR = 128
TMINE = 1024
TBLK = 128
TC = 1024
NCH = T // TC
NH, HPC = 16, 2
DN, DR, DV = 128, 64, 128
RQ, RKV = 1536, 512
DQK = DN + DR
QT = 512
NQT = T // QT
ATTN_SCALE = 1.0 / float(np.sqrt(DQK))
V_SCALE = 1.0 / float(np.sqrt(NH * DV))

PHASE_LIMIT = 4
DEBUG_BITNET = False
DEBUG_ATTN = False
ATTN_STEP = 0
ATTN_RB = 12


class Sem:
    __slots__ = ("h", "n")

    def __init__(self, h):
        self.h = h
        self.n = 0

    def inc(self, k=1):
        self.n += k
        return self.n


def _declare_tensors(nc, H):
    def inp(name, shape, dtype=F32):
        H[name] = nc.dram_tensor(name, list(shape), dtype, kind="ExternalInput").ap()

    def outp(name, shape, dtype=F32):
        H[name] = nc.dram_tensor(name, list(shape), dtype, kind="ExternalOutput").ap()

    def internal(name, shape, dtype):
        H[name] = nc.dram_tensor(name, list(shape), dtype, kind="Internal").ap()

    inp("xT", (DM, T))
    inp("x_res", (DM, TMINE))
    inp("inpT", (DM, 1024))
    inp("conv_w", (DIC, 4))
    inp("conv_b", (DIC, 1))
    inp("xpT", (DIC, 160))
    inp("dtT", (DTR, DIC))
    inp("dt_b", (DIC, 1))
    inp("Alog", (DIC, NST))
    inp("Dvec", (DIC, 1))
    inp("outT", (DI, DM))
    inp("l0w_T", (DM, DM))
    inp("l0_gamma", (DM, 1))
    inp("cq", (DM, RQ))
    inp("q_gamma", (RQ, 1))
    inp("dq_n", (RQ, HPC * DN))
    inp("dq_r", (RQ, HPC * DR))
    inp("dq_r2", (RQ, HPC * DR))
    inp("ckv", (DM, RKV))
    inp("kv_gamma", (RKV, 1))
    inp("dk_n", (RKV, HPC * DN))
    inp("dv_w", (RKV, HPC * DV))
    inp("krope_w", (DM, DR))
    inp("krope_w2", (DM, DR))
    inp("cos2h", (HPC * DR, 1))
    inp("sin2h", (HPC * DR, 1))
    inp("sgnrope", (HPC * DR, 1))
    inp("wproj", (DM, DM))
    inp("moe_wr", (DM, 8))
    inp("moe_br", (8, 1))
    inp("moe_a", (DM, 512))
    inp("moe_b", (512, DM))
    inp("selmat", (8, 512), BF16)
    inp("identb", (128, 128), BF16)
    inp("onescol", (128, 1), BF16)
    inp("masks", (128, 4 * QT), BF16)
    inp("coff", (8, 1), I32)

    outp("out", (DM, TMINE))
    if DEBUG_BITNET:
        outp("dbg", (128, 2048))
        outp("dbgb", (128, 49152), BF16)

    internal("sp_delta", (NCH, 128 * 4096), BF16)
    internal("sp_dx", (NCH, 128 * 4096), BF16)
    internal("sp_xconv", (NCH, 128 * 4096), BF16)
    internal("sp_zs", (NCH, 128 * 4096), BF16)
    internal("ar_in", (2, 160 * TC), F32)
    internal("ar_out", (NCH, 160 * TC), F32)
    internal("y_in", (NCH, DIC * TC), BF16)
    internal("y_out", (NCH, NC8 * DIC * TC), BF16)
    internal("mx_in", (128,), F32)
    internal("mx_out", (128,), F32)
    internal("wq_sp", (DM * DM,), BF16)
    internal("nq_in", (RQ * TMINE,), BF16)
    internal("nq_out", (NC8 * RQ * TMINE,), BF16)
    internal("kv_in", ((RKV + 2 * DR) * TMINE,), BF16)
    internal("kv_out", (NC8 * (RKV + 2 * DR) * TMINE,), BF16)
    internal("x2_sp", (DM * TMINE,), F32)
    internal("ao_in", (HPC * DN * T,), BF16)
    internal("ao_out", (NC8 * HPC * DN * T,), BF16)
    if DEBUG_ATTN:
        outp("dbga", (128, 6 * 16384), BF16)


def build_program():
    nc = bass.Bass("TRN2", target_bir_lowering=False, debug=False)
    H = {}
    _declare_tensors(nc, H)
    with contextlib.ExitStack() as es:
        p = Plan(nc, H, es)
        p.run()
    return nc, p.meas


_BUILD_CACHE = {}


def build_program2():
    key = (PHASE_LIMIT, DEBUG_BITNET, DEBUG_ATTN, ATTN_STEP, ATTN_RB)
    if key in _BUILD_CACHE:
        return _BUILD_CACHE[key]
    _, meas = build_program()
    nc = bass.Bass("TRN2", target_bir_lowering=False, debug=False)
    H = {}
    _declare_tensors(nc, H)
    with contextlib.ExitStack() as es:
        p = Plan(nc, H, es, actual=meas)
        p.run()
        for k, v in p.meas.items():
            assert p.actual.get(k) == v, ("unstable dma split", k, p.actual.get(k), v)
    _BUILD_CACHE[key] = nc
    return nc


class Plan:
    def __init__(self, nc, H, es, actual=None):
        self.nc = nc
        self.H = H
        self.es = es
        self.streams = {k: [] for k in ("sync", "vector", "scalar", "tensor", "gpsimd")}
        self.sems = {}
        self.d = {}
        self.actual = actual or {}
        self.meas = {}
        self._emitcnt = {}

    def dma(self, e, sem, out, in_, **kw):
        bb = self.nc.m.functions[0].blocks[-1].instructions
        n0 = len(bb)
        inst = e.dma_start(out=out, in_=in_, **kw)
        inst.then_inc(sem.h, 16)
        add = 0
        for i in bb[n0:]:
            si = i.sync_info
            if si is not None and si.on_update:
                for u in si.on_update:
                    if u.id == sem.h.num and u.update_value:
                        add += u.update_value
        self._emitcnt[sem.h.num] = self._emitcnt.get(sem.h.num, 0) + add
        return inst

    def mark(self, eng, key, sem, planned):
        self.d[key] = self.actual.get(key, planned)
        def rec(e, key=key, sem=sem):
            self.meas[key] = self._emitcnt.get(sem.h.num, 0)
        self.em(eng, rec)

    def em(self, eng, fn):
        self.streams[eng].append(fn)

    def sem(self, name):
        if name not in self.sems:
            self.sems[name] = Sem(self.es.enter_context(self.nc.semaphore(name)))
        return self.sems[name]

    SBUF_BASE = 16384 + 4096

    def sb(self, name, off, rows, cols, dtype):
        esz = 2 if dtype == BF16 else 4
        if dtype == I32:
            esz = 4
        off = off + self.SBUF_BASE
        assert off % 32 == 0 and off + cols * esz <= 229368, (name, off, cols)
        return self.nc.alloc_sbuf_tensor_at(name, [rows, cols], dtype, offset=off)

    def vedge(self, v, inst):
        vv = self.sem("vv")
        inst.then_inc(vv.h, 1)
        vv.inc()
        v.wait_ge(vv.h, vv.n)

    def sedge(self, s, inst):
        ss = self.sem("sse")
        inst.then_inc(ss.h, 1)
        ss.inc()
        s.wait_ge(ss.h, ss.n)

    def run(self):
        self.alloc_shared()
        self.P = [self.es.enter_context(
            self.nc.psum_tensor(f"P{i}", [128, 1024], F32)) for i in range(4)]
        self.plan_const()
        if PHASE_LIMIT == -3:
            self.plan_fake_ln2()
            self.plan_attn()
            self.plan_out()
        else:
            self.plan_mamba()
            if PHASE_LIMIT >= 2:
                self.plan_bitnet()
            if PHASE_LIMIT >= 3:
                self.plan_attn()
            if PHASE_LIMIT >= 4:
                self.plan_moe()
            self.plan_out()
        block = self.es.enter_context(self.nc.Block())
        for eng in ("sync", "vector", "scalar", "tensor", "gpsimd"):
            fns = self.streams[eng]
            getattr(block, eng)(lambda e, fns=fns: [f(e) for f in fns])

    def alloc_shared(self):
        S = self.sb
        self.acol = S("acol", 0, 128, 64, F32)
        self.dcol = S("dcol", 256, 128, 4, F32)
        self.dtb = S("dtb", 288, 128, 4, F32)
        self.cvb = S("cvb", 320, 128, 4, F32)
        self.carry = S("carry", 352, 128, 64, F32)
        self.identb = S("identb_t", 608, 128, 128, BF16)
        self.onesb = S("onesb_t", 864, 128, 1, BF16)
        self.onesf = S("onesf_t", 896, 128, 1, F32)
        self.rowa = S("rowa", 928, 128, 1024, F32)
        self.rowb = S("rowb", 5024, 128, 1024, F32)
        self.gma = S("gma", 9120, 128, 16, F32)
        self.convd = S("convd", 9184, 128, 16 * 128, BF16)
        self.masks = S("masks_t", 13280, 128, 4 * QT, BF16)
        self.qgam = S("qgam", 17376, 128, 12, F32)
        self.kvgam = S("kvgam", 17440, 128, 4, F32)
        self.cosc = S("cosc", 17472, 128, 1, F32)
        self.sinc = S("sinc", 17504, 128, 1, F32)
        self.nsinc = S("nsinc", 17536, 128, 1, F32)
        self.misc = S("misc", 17568, 128, 128, F32)
        self.coffs = S("coffs", 18080, 8, 1, I32)
        self.onesr = S("onesr", 18112, 1, 128, F32)
        self.sgn = S("sgnr", 19712, 128, 1, F32)
        self.cscs = S("cscs", 19744, 128, 1, F32)
        self.nscs = S("nscs", 19776, 128, 1, F32)
        self.stream = S("stream", 20 * 1024, 128, 16 * 1024, F32)
        self.streamb = S("streamb", 84 * 1024, 128, 16 * 1024, BF16)
        self.B0 = 116 * 1024

    def stile(self, k):
        return self.stream[:, k * 1024:(k + 1) * 1024]

    def sbtile(self, k):
        return self.streamb[:, k * 1024:(k + 1) * 1024]

    def plan_const(self):
        H = self.H
        ld = self.sem("c_ld")
        cd = self.sem("c_done")

        def col3(dst, ncols):
            return dst.rearrange("p (a n) -> p a n", n=ncols)

        def dmas(sync):
            if PHASE_LIMIT == -3 and ATTN_STEP >= 123:
                return
            for dst, src in [
                (col3(self.acol[:, 0:64], 16),
                 H["Alog"].rearrange("(a p) n -> p a n", p=128)),
                (col3(self.dcol[:], 1), H["Dvec"].rearrange("(a p) o -> p a o", p=128)),
                (col3(self.dtb[:], 1), H["dt_b"].rearrange("(a p) o -> p a o", p=128)),
                (col3(self.cvb[:], 1), H["conv_b"].rearrange("(a p) o -> p a o", p=128)),
                (self.identb[:], H["identb"]),
                (self.onesb[:], H["onescol"]),
                (self.masks[:], H["masks"]),
                (col3(self.gma[:], 1), H["l0_gamma"].rearrange("(a p) o -> p a o", p=128)),
                (col3(self.qgam[:], 1), H["q_gamma"].rearrange("(a p) o -> p a o", p=128)),
                (col3(self.kvgam[:], 1),
                 H["kv_gamma"].rearrange("(a p) o -> p a o", p=128)),
                (self.cosc[:], H["cos2h"]),
                (self.sinc[:], H["sin2h"]),
                (self.sgn[:], H["sgnrope"]),
                (col3(self.rowa[:, 0:16], 4),
                 H["conv_w"].rearrange("(a p) k -> p a k", p=128)),
                (self.coffs[:], H["coff"]),
            ]:
                self.dma(sync, ld, dst, src, allow_slow_non_contiguous=True)
        self.em("sync", dmas)
        if not (PHASE_LIMIT == -3 and ATTN_STEP >= 123):
            ld.inc(15 * 16)
        self.mark("sync", "c_ld_done", ld, ld.n)

        def swork(s):
            if PHASE_LIMIT == -3 and ATTN_STEP >= 121:
                s.nop().then_inc(cd.h, 1)
                return
            s.wait_ge(ld.h, self.d["c_ld_done"])
            s.activation(out=self.acol[:, 0:64], in_=self.acol[:, 0:64],
                         func=ACT.Exp).then_inc(cd.h, 1)
        self.em("scalar", swork)
        cd.inc()
        exp_done = cd.n

        def vwork(v):
            if PHASE_LIMIT == -3 and ATTN_STEP >= 121:
                v.nop().then_inc(cd.h, 1)
                return
            v.wait_ge(cd.h, exp_done)
            v.tensor_scalar(out=self.acol[:, 0:64], in0=self.acol[:, 0:64],
                            scalar1=-1.0, scalar2=None, op0=ALU.mult)
            i0 = v.tensor_scalar(out=self.nsinc[:], in0=self.sinc[:],
                                 scalar1=self.sgn[:, 0:1], scalar2=None,
                                 op0=ALU.mult)
            v.tensor_scalar(out=self.cscs[:], in0=self.cosc[:],
                            scalar1=ATTN_SCALE, scalar2=None, op0=ALU.mult)
            self.vedge(v, i0)
            v.tensor_scalar(out=self.nscs[:], in0=self.nsinc[:],
                            scalar1=ATTN_SCALE, scalar2=None, op0=ALU.mult)
            v.memset(self.onesf[:], 1.0)
            v.memset(self.misc[:], 0.0)
            v.memset(self.onesr[:], 1.0)
            v.memset(self.carry[:], 0.0)
            last = None
            for dt in range(4):
                for k in range(4):
                    last = v.tensor_scalar(
                        out=self.convd[:, (dt * 4 + k) * 128:(dt * 4 + k + 1) * 128],
                        in0=self.identb[:],
                        scalar1=self.rowa[:, dt * 4 + k:dt * 4 + k + 1],
                        scalar2=None, op0=ALU.mult)
            last.then_inc(cd.h, 1)
        self.em("vector", vwork)
        cd.inc()
        self.c_done = cd

    def plan_mamba(self):
        B = self.B0
        S = self.sb
        nc = self.nc
        self.mm_Wq = S("m_Wq", 20 * 1024, 128, 16 * 1024, BF16)
        self.mm_xt = S("m_xt", 52 * 1024, 128, 16 * 1024, BF16)
        self.mm_xzx = S("m_xzx", B, 128, 4 * 1040, BF16)
        self.mm_xcv = S("m_xcv", B + 8512, 128, 4096, BF16)
        self.mm_zs = S("m_zs", B + 16704, 128, 4096, BF16)
        self.mm_dxc = S("m_dxc", B + 24896, 128, 4096, BF16)
        self.mm_dlt = S("m_dlt", B + 33088, 128, 4096, BF16)
        self.mm_e = S("m_e", B + 41280, 128, 1024, F32)
        self.mm_xdl = S("m_xdl", B + 45376, 128, 1024, F32)
        self.mm_xdbc = S("m_xdbc", B + 49472, 128, 1024, F32)
        self.mm_dtl = S("m_dtl", B + 53568, 128, 1024, BF16)
        self.mm_wsm = S("m_wsm", B + 55616, 128, 1152, BF16)

        for c in range(NCH):
            self.mamba_pre_chunk(c)
        self.mamba_scan()
        self.mamba_post()

    def mamba_pre_chunk(self, c):
        H, em, P, d = self.H, self.em, self.P, self.d
        ld = self.sem("m_ld")
        gld = self.sem("m_gld")
        te = self.sem("m_te")
        ve = self.sem("m_ve")
        se = self.sem("m_se")
        ar = self.sem("m_ar")
        sp = self.sem("m_sp")
        Wq, xt = self.mm_Wq, self.mm_xt
        xzx, xcv, zs = self.mm_xzx, self.mm_xcv, self.mm_zs
        dxc, dlt, ebuf = self.mm_dxc, self.mm_dlt, self.mm_e
        xdl, xdbc, dtl, wsm = self.mm_xdl, self.mm_xdbc, self.mm_dtl, self.mm_wsm
        cdn = self.c_done.n

        if c == 0:
            def w_loads(g):
                for k in range(16):
                    self.dma(g, gld, Wq[:, k * 1024:(k + 1) * 1024],
                             H["inpT"][k * 128:(k + 1) * 128, :])
                for k in range(4):
                    self.dma(g, gld, wsm[:, k * 160:(k + 1) * 160],
                             H["xpT"][k * 128:(k + 1) * 128, :])
                self.dma(g, gld, wsm[:, 640:1152], H["dtT"])
            em("gpsimd", w_loads)
            gld.inc(21 * 16)
            self.mark("gpsimd", "m_wq", gld, gld.n)

        def load_xt(g, c=c):
            if c > 0:
                g.wait_ge(te.h, d[f"m_tez{c-1}"])
            for k in range(16):
                self.dma(g, gld, xt[:, k * 1024:(k + 1) * 1024],
                         H["xT"][k * 128:(k + 1) * 128, c * TC:(c + 1) * TC])
        em("gpsimd", load_xt)
        gld.inc(16 * 16)
        self.mark("gpsimd", f"m_xt{c}", gld, gld.n)

        def mm_inproj(t, c=c):
            t.wait_ge(gld.h, max(d[f"m_xt{c}"], d["m_wq"]))
            if c == 0:
                t.wait_ge(self.c_done.h, cdn)
            for half in range(2):
                if half == 1:
                    t.wait_ge(ve.h, d[f"m_evx{c}"])
                elif c > 0:
                    t.wait_ge(se.h, d[f"m_dltr{c-1}"])
                last = None
                for k in range(16):
                    for o in range(4):
                        oc = half * 4 + o
                        for t2 in range(2):
                            last = t.matmul(
                                P[o][:, t2 * 512:(t2 + 1) * 512],
                                Wq[:, k * 1024 + oc * 128:k * 1024 + (oc + 1) * 128],
                                xt[:, k * 1024 + t2 * 512:k * 1024 + (t2 + 1) * 512],
                                start=(k == 0), stop=(k == 15))
                last.then_inc(te.h, 1)
        em("tensor", mm_inproj)
        te.inc(2)
        d[f"m_tex{c}"], d[f"m_tez{c}"] = te.n - 1, te.n

        def ev_x(v, c=c):
            v.wait_ge(te.h, d[f"m_tex{c}"])
            if c > 0:
                v.wait_ge(te.h, d[f"m_conv{c-1}"])
                v.wait_ge(ve.h, d[f"m_tailmv{c-1}"])
            last = None
            for o in range(4):
                last = v.tensor_copy(out=xzx[:, o * 1040 + 3:o * 1040 + 1027],
                                     in_=P[o][:])
            if c == 0:
                for o in range(4):
                    last = v.memset(xzx[:, o * 1040:o * 1040 + 3], 0.0)
            last.then_inc(ve.h, 1)
        em("vector", ev_x)
        ve.inc()
        d[f"m_evx{c}"] = ve.n

        def tailcp(v, c=c):
            v.wait_ge(ve.h, d[f"m_evx{c}"])
            if c == 0:
                v.wait_ge(self.c_done.h, self.c_done.n)
            last = None
            for o in range(4):
                last = v.tensor_copy(out=self.misc[:, 4 + o * 4:7 + o * 4],
                                     in_=xzx[:, o * 1040 + 1024:o * 1040 + 1027])
            last.then_inc(ve.h, 1)
        em("vector", tailcp)
        ve.inc()
        d[f"m_tail{c}"] = ve.n

        if c > 0:
            def halo(v, c=c):
                v.wait_ge(ve.h, d[f"m_tailmv{c-1}"])
                last = None
                for o in range(4):
                    last = v.tensor_copy(out=xzx[:, o * 1040:o * 1040 + 3],
                                         in_=self.misc[:, 20 + o * 4:23 + o * 4])
                last.then_inc(ve.h, 1)
            em("vector", halo)
            ve.inc()
            d[f"m_halo{c}"] = ve.n

        def tailmv(v, c=c):
            v.wait_ge(ve.h, d[f"m_halo{c}"] if c > 0 else d[f"m_tail{c}"])
            v.tensor_copy(out=self.misc[:, 20:36],
                          in_=self.misc[:, 4:20]).then_inc(ve.h, 1)
        em("vector", tailmv)
        ve.inc()
        d[f"m_tailmv{c}"] = ve.n

        def ev_z(s, c=c):
            s.wait_ge(te.h, d[f"m_tez{c}"])
            if c > 0:
                s.wait_ge(sp.h, d[f"m_sp{c-1}"])
            last = None
            for o in range(4):
                last = s.activation(out=zs[:, o * 1024:(o + 1) * 1024], in_=P[o][:],
                                    func=ACT.Silu)
            last.then_inc(se.h, 1)
        em("scalar", ev_z)
        se.inc()
        d[f"m_zs{c}"] = se.n

        def mm_conv(t, c=c):
            t.wait_ge(ve.h, d[f"m_tailmv{c}"])
            t.wait_ge(se.h, d[f"m_zs{c}"])
            last = None
            for o in range(4):
                for k in range(4):
                    for t2 in range(2):
                        last = t.matmul(
                            P[o][:, t2 * 512:(t2 + 1) * 512],
                            self.convd[:, (o * 4 + k) * 128:(o * 4 + k + 1) * 128],
                            xzx[:, o * 1040 + k + t2 * 512:o * 1040 + k + t2 * 512 + 512],
                            start=(k == 0), stop=(k == 3))
            last.then_inc(te.h, 1)
        em("tensor", mm_conv)
        te.inc()
        d[f"m_conv{c}"] = te.n

        def ev_conv(s, c=c):
            s.wait_ge(te.h, d[f"m_conv{c}"])
            if c > 0:
                s.wait_ge(sp.h, d[f"m_sp{c-1}"])
                s.wait_ge(ve.h, d[f"m_dx{c-1}"])
            last = None
            for o in range(4):
                last = s.activation(out=xcv[:, o * 1024:(o + 1) * 1024], in_=P[o][:],
                                    func=ACT.Silu, bias=self.cvb[:, o:o + 1])
            last.then_inc(se.h, 1)
        em("scalar", ev_conv)
        se.inc()
        d[f"m_xcv{c}"] = se.n

        def mm_xdbl(t, c=c):
            t.wait_ge(se.h, d[f"m_xcv{c}"])
            last = None
            for k in range(4):
                for t2 in range(2):
                    rhs = xcv[:, k * 1024 + t2 * 512:k * 1024 + t2 * 512 + 512]
                    t.matmul(P[0][:, t2 * 512:(t2 + 1) * 512],
                             wsm[:, k * 160:k * 160 + 128], rhs,
                             start=(k == 0), stop=(k == 3))
                    last = t.matmul(P[1][0:32, t2 * 512:(t2 + 1) * 512],
                                    wsm[:, k * 160 + 128:k * 160 + 160], rhs,
                                    start=(k == 0), stop=(k == 3))
            last.then_inc(te.h, 1)
        em("tensor", mm_xdbl)
        te.inc()
        d[f"m_xdbl{c}"] = te.n

        def ev_xdbl(v, c=c):
            v.wait_ge(te.h, d[f"m_xdbl{c}"])
            if c > 0:
                v.wait_ge(ld.h, d[f"m_xst{c-1}"])
            v.tensor_copy(out=xdl[:], in_=P[0][:])
            v.tensor_copy(out=xdbc[0:32, :], in_=P[1][0:32, :]).then_inc(ve.h, 1)
        em("vector", ev_xdbl)
        ve.inc()
        d[f"m_xdev{c}"] = ve.n

        def st_xdbl(sy, c=c):
            sy.wait_ge(ve.h, d[f"m_xdev{c}"])
            if c >= 2:
                sy.wait_ge(ar.h, d[f"m_ar{c-2}"])
            buf = c % 2
            self.dma(sy, ld, H["ar_in"][buf, 0:128 * 1024]
                     .rearrange("(p n) -> p n", p=128), xdl[:])
            self.dma(sy, ld, H["ar_in"][buf, 128 * 1024:160 * 1024]
                     .rearrange("(p n) -> p n", p=32), xdbc[0:32, :])
        em("sync", st_xdbl)
        ld.inc(32)
        self.mark("sync", f"m_xst{c}", ld, ld.n)

        def trig_ar(g, c=c):
            g.wait_ge(ld.h, d[f"m_xst{c}"])
            g.collective_compute(
                "AllReduce", ALU.add, replica_groups=[list(range(NC8))],
                ins=[H["ar_in"][c % 2]], outs=[H["ar_out"][c]],
            ).then_inc(ar.h, 1)
        em("gpsimd", trig_ar)
        ar.inc()
        d[f"m_ar{c}"] = ar.n

        gdt = self.sem("m_gdt")

        def rd_dtlo(g, c=c):
            g.wait_ge(ar.h, d[f"m_ar{c}"])
            if c > 0:
                g.wait_ge(te.h, d[f"m_dt{c-1}"])
            self.dma(g, gdt, dtl[:],
                     H["ar_out"][c, 0:128 * 1024].rearrange("(p n) -> p n", p=128))
        em("gpsimd", rd_dtlo)
        gdt.inc(16)
        self.mark("gpsimd", f"m_dtlo{c}", gdt, gdt.n)

        def mm_dt(t, c=c):
            t.wait_ge(gdt.h, d[f"m_dtlo{c}"])
            t.wait_ge(ve.h, d[f"m_xdev{c}"])
            last = None
            for o in range(4):
                for t2 in range(2):
                    last = t.matmul(P[o][:, t2 * 512:(t2 + 1) * 512],
                                    wsm[:, 640 + o * 128:640 + (o + 1) * 128],
                                    dtl[:, t2 * 512:(t2 + 1) * 512],
                                    start=True, stop=True)
            last.then_inc(te.h, 1)
        em("tensor", mm_dt)
        te.inc()
        d[f"m_dt{c}"] = te.n

        def sp_softplus(s, c=c):
            s.wait_ge(te.h, d[f"m_dt{c}"])
            if c > 0:
                s.wait_ge(ve.h, d[f"m_dx{c-1}"])
                s.wait_ge(sp.h, d[f"m_sp{c-1}"])
            last = None
            for o in range(4):
                i1 = s.activation(out=ebuf[:], in_=P[o][:], func=ACT.Exp,
                                  bias=self.dtb[:, o:o + 1])
                self.sedge(s, i1)
                last = s.activation(out=dlt[:, o * 1024:(o + 1) * 1024], in_=ebuf[:],
                                    func=ACT.Ln, bias=1.0)
                self.sedge(s, last)
            s.sem_inc(se.h, 1)
        em("scalar", sp_softplus)
        se.inc()
        d[f"m_dltr{c}"] = se.n

        def mk_dx(v, c=c):
            v.wait_ge(se.h, d[f"m_dltr{c}"])
            if c > 0:
                v.wait_ge(sp.h, d[f"m_sp{c-1}"])
            v.tensor_tensor(out=dxc[:], in0=dlt[:], in1=xcv[:],
                            op=ALU.mult).then_inc(ve.h, 1)
        em("vector", mk_dx)
        ve.inc()
        d[f"m_dx{c}"] = ve.n

        def spills(sy, c=c):
            sy.wait_ge(ve.h, d[f"m_dx{c}"])
            sy.wait_ge(se.h, d[f"m_zs{c}"])
            for nm, t_ in (("sp_delta", dlt), ("sp_dx", dxc),
                           ("sp_xconv", xcv), ("sp_zs", zs)):
                self.dma(sy, sp, H[nm][c].rearrange("(p n) -> p n", p=128), t_[:])
        em("sync", spills)
        sp.inc(64)
        self.mark("sync", f"m_sp{c}", sp, sp.n)

    def mamba_scan(self):
        H, em, d, P = self.H, self.em, self.d, self.P
        S = self.sb
        B = self.B0
        dlt2 = [self.mm_dlt, S("s_dlt1", B + 41280, 128, 4096, BF16)]
        dxc2 = [self.mm_dxc, S("s_dxc1", B + 49472, 128, 4096, BF16)]
        xcv2 = [self.mm_xcv, self.mm_zs]
        zs2 = [S("s_zs0", 20 * 1024, 128, 4096, BF16),
               S("s_zs1", 28 * 1024 + 192, 128, 4096, BF16)]
        abuf = [S("s_a0", 36 * 1024 + 384, 128, 4096, F32),
                S("s_a1", 52 * 1024 + 384, 128, 4096, F32)]
        bbuf = [S("s_b0", 68 * 1024 + 384, 128, 4096, BF16),
                S("s_b1", 76 * 1024 + 384, 128, 4096, BF16)]
        hbuf = [S("s_h0", 84 * 1024 + 384, 128, 4096, BF16),
                S("s_h1", 92 * 1024 + 384, 128, 4096, BF16)]
        cpb = [S("s_cp0", 100 * 1024 + 384, 128, 4096, BF16),
               S("s_cp1", 108 * 1024 + 384, 128, 4096, BF16)]
        brep = [S(f"s_br{i}", 184640 + 2048 * i, 128, 1024, BF16) for i in range(2)]
        crep = [S(f"s_cr{i}", 188736 + 2048 * i, 128, 1024, BF16) for i in range(2)]
        y2b = S("s_y2", B + 8512, 128, 4096, BF16)
        y2b = S("s_y2b", B, 128, 4096, BF16)
        atm = self.misc[:, 40:44]
        ftm = self.misc[:, 44:48]

        ldp = [self.sem("s_ld0"), self.sem("s_ld1")]
        gldp = [self.sem("s_gld0"), self.sem("s_gld1")]
        ve = self.sem("s_ve")
        se = self.sem("s_se")
        te = self.sem("s_te")
        ag = self.sem("s_ag")
        yst = self.sem("s_yst")

        def key(g):
            return divmod(g, NST)

        for c in range(NCH):
            cb = c % 2

            def rb(sy, c=c, cb=cb):
                sy.wait_ge(self.sems["m_sp"].h, d[f"m_sp{NCH-1}"])
                if c >= 2:
                    sy.wait_ge(ve.h, d[f"s_gate{c-2}"])
                for dst, nm in ((dlt2[cb], "sp_delta"), (dxc2[cb], "sp_dx"),
                                (xcv2[cb], "sp_xconv"), (zs2[cb], "sp_zs")):
                    self.dma(sy, ldp[cb], dst[:],
                             H[nm][c].rearrange("(p n) -> p n", p=128))
            em("sync", rb)
            ldp[cb].inc(64)
            self.mark("sync", f"s_rb{c}", ldp[cb], ldp[cb].n)

            for n in range(NST):
                g = c * NST + n
                nb = n % 2

                def bc_ld(gp, c=c, n=n, nb=nb, g=g):
                    if n == 0:
                        gp.wait_ge(self.sems["m_ar"].h, d[f"m_ar{c}"])
                    if g >= 2:
                        c2, n2 = key(g - 2)
                        gp.wait_ge(ve.h, d[f"s_cp{c2}_{n2}"])
                    self.dma(gp, gldp[g % 2], brep[nb][:],
                             H["ar_out"][c, (128 + n) * 1024:(129 + n) * 1024]
                             .unsqueeze(0).to_broadcast([128, 1024]))
                    self.dma(gp, gldp[g % 2], crep[nb][:],
                             H["ar_out"][c, (144 + n) * 1024:(145 + n) * 1024]
                             .unsqueeze(0).to_broadcast([128, 1024]))
                em("gpsimd", bc_ld)
                gldp[g % 2].inc(32)
                self.mark("gpsimd", f"s_bc{c}_{n}", gldp[g % 2], gldp[g % 2].n)

                def aexp(s, c=c, n=n, nb=nb, cb=cb, g=g):
                    if n == 0:
                        s.wait_ge(ldp[cb].h, d[f"s_rb{c}"])
                    if g >= 2:
                        c2, n2 = key(g - 2)
                        s.wait_ge(ve.h, d[f"s_scan{c2}_{n2}"])
                    last = None
                    for j in range(4):
                        last = s.activation(
                            out=abuf[nb][:, j * 1024:(j + 1) * 1024],
                            in_=dlt2[cb][:, j * 1024:(j + 1) * 1024],
                            func=ACT.Exp,
                            scale=self.acol[:, j * 16 + n:j * 16 + n + 1])
                    last.then_inc(se.h, 1)
                em("scalar", aexp)
                se.inc()
                d[f"s_ae{c}_{n}"] = se.n

                def bwork(v, c=c, n=n, nb=nb, cb=cb, g=g):
                    v.wait_ge(gldp[g % 2].h, d[f"s_bc{c}_{n}"])
                    if n == 0:
                        v.wait_ge(ldp[cb].h, d[f"s_rb{c}"])
                    if g == 0:
                        v.wait_ge(self.c_done.h, self.c_done.n)
                    if g >= 2:
                        c2, n2 = key(g - 2)
                        v.wait_ge(ve.h, d[f"s_scan{c2}_{n2}"])
                    last = None
                    for j in range(4):
                        last = v.tensor_tensor(
                            out=bbuf[nb][:, j * 1024:(j + 1) * 1024],
                            in0=dxc2[cb][:, j * 1024:(j + 1) * 1024],
                            in1=brep[nb][:], op=ALU.mult)
                    last.then_inc(ve.h, 1)
                em("vector", bwork)
                ve.inc()
                d[f"s_bmul{c}_{n}"] = ve.n

                def bfix(v, c=c, n=n, nb=nb):
                    v.wait_ge(se.h, d[f"s_ae{c}_{n}"])
                    v.wait_ge(ve.h, d[f"s_bmul{c}_{n}"])
                    a3 = abuf[nb].ap().rearrange("p (j t) -> p j t", j=4)
                    b3 = bbuf[nb].ap().rearrange("p (j t) -> p j t", j=4)
                    i1 = v.tensor_tensor(out=ftm, in0=a3[:, :, 0],
                                         in1=self.carry[:, 4 * n:4 * n + 4],
                                         op=ALU.mult)
                    self.vedge(v, i1)
                    v.memset(a3[:, :, 0], 0.0)
                    v.tensor_tensor(out=b3[:, :, 0], in0=b3[:, :, 0], in1=ftm,
                                    op=ALU.add).then_inc(ve.h, 1)
                em("vector", bfix)
                ve.inc()
                d[f"s_bfix{c}_{n}"] = ve.n

                def scan(v, c=c, n=n, nb=nb):
                    v.wait_ge(ve.h, d[f"s_bfix{c}_{n}"])
                    i1 = v.tensor_tensor_scan(out=hbuf[nb][:], data0=abuf[nb][:],
                                              data1=bbuf[nb][:], initial=0.0,
                                              op0=ALU.mult, op1=ALU.add)
                    self.vedge(v, i1)
                    h3 = hbuf[nb].ap().rearrange("p (j t) -> p j t", j=4)
                    v.tensor_copy(out=self.carry[:, 4 * n:4 * n + 4],
                                  in_=h3[:, :, 1023]).then_inc(ve.h, 1)
                em("vector", scan)
                ve.inc()
                d[f"s_scan{c}_{n}"] = ve.n

                def cpw(v, c=c, n=n, nb=nb, g=g):
                    v.wait_ge(ve.h, d[f"s_scan{c}_{n}"])
                    if g >= 2:
                        c2, n2 = key(g - 2)
                        v.wait_ge(te.h, d[f"s_ys{c2}_{n2}"])
                    last = None
                    for j in range(4):
                        last = v.tensor_tensor(
                            out=cpb[nb][:, j * 1024:(j + 1) * 1024],
                            in0=hbuf[nb][:, j * 1024:(j + 1) * 1024],
                            in1=crep[nb][:], op=ALU.mult)
                    last.then_inc(ve.h, 1)
                em("vector", cpw)
                ve.inc()
                d[f"s_cp{c}_{n}"] = ve.n

                def ysum(t, c=c, n=n, nb=nb):
                    t.wait_ge(ve.h, d[f"s_cp{c}_{n}"])
                    if n == 0 and c > 0:
                        t.wait_ge(ve.h, d[f"s_gate{c-1}"])
                    last = None
                    for j in range(4):
                        for t2 in range(2):
                            last = t.matmul(
                                P[j][:, t2 * 512:(t2 + 1) * 512],
                                self.identb[:],
                                cpb[nb][:, j * 1024 + t2 * 512:j * 1024 + t2 * 512 + 512],
                                start=(n == 0), stop=(n == NST - 1))
                    last.then_inc(te.h, 1)
                em("tensor", ysum)
                te.inc()
                d[f"s_ys{c}_{n}"] = te.n

            def gate(v, c=c, cb=cb):
                v.wait_ge(te.h, d[f"s_ys{c}_{NST-1}"])
                v.wait_ge(ldp[cb].h, d[f"s_rb{c}"])
                if c > 0:
                    v.wait_ge(yst.h, d[f"s_yst{c-1}"])
                last = None
                for j in range(4):
                    last = v.scalar_tensor_tensor(
                        out=y2b[:, j * 1024:(j + 1) * 1024],
                        in0=xcv2[cb][:, j * 1024:(j + 1) * 1024],
                        scalar=self.dcol[:, j:j + 1],
                        in1=P[j][:], op0=ALU.mult, op1=ALU.add)
                self.vedge(v, last)
                v.tensor_tensor(out=y2b[:], in0=y2b[:], in1=zs2[cb][:],
                                op=ALU.mult).then_inc(ve.h, 1)
            em("vector", gate)
            ve.inc()
            d[f"s_gate{c}"] = ve.n

            def st_y(sy, c=c):
                sy.wait_ge(ve.h, d[f"s_gate{c}"])
                self.dma(sy, yst,
                         H["y_in"][c].rearrange("(j p t) -> p j t", j=4, p=128),
                         y2b[:].rearrange("p (j t) -> p j t", j=4))
            em("sync", st_y)
            yst.inc(16)
            self.mark("sync", f"s_yst{c}", yst, yst.n)

            def trig_ag(g_, c=c):
                g_.wait_ge(yst.h, d[f"s_yst{c}"])
                g_.collective_compute(
                    "AllGather", ALU.bypass, replica_groups=[list(range(NC8))],
                    ins=[H["y_in"][c]], outs=[H["y_out"][c]],
                ).then_inc(ag.h, 1)
            em("gpsimd", trig_ag)
            ag.inc()
            d[f"s_ag{c}"] = ag.n

    def mamba_post(self):
        H, em, d, P = self.H, self.em, self.d, self.P
        S = self.sb
        yr = S("o_yr", 84 * 1024, 128, 32 * 1024, BF16)
        wt = [S(f"o_w{i}", 151552 + i * 512, 128, 256, BF16) for i in range(2)]
        xr = [S(f"o_xr{i}", 152576 + i * 4096, 128, 1024, F32)
              for i in range(2)]
        ld = self.sem("o_ld")
        lda = self.sem("o_lda")
        ldg = self.sem("o_ldg")
        xlp = [self.sem("o_xld0"), self.sem("o_xld1")]
        gldp = [self.sem("o_gld0"), self.sem("o_gld1")]
        te = self.sem("o_te")
        ve = self.sem("o_ve")

        yo5 = H["y_out"].rearrange("c (s j p t) -> c s j p t", s=8, j=4, p=128)

        def mk_gather(eng_senders, sem_, key, eng):
            def gather_y(e, eng_senders=eng_senders):
                e.wait_ge(self.sems["c_ld"].h, d["c_ld_done"])
                e.wait_ge(self.sems["s_ag"].h, d[f"s_ag{NCH-1}"])
                me = e.value_load(self.coffs[0:1, 0:1])
                off = me * 128
                for s in eng_senders:
                    for j in range(4):
                        k = s * 4 + j
                        sap = yo5[:, s, j, :, :][:, :, bass.DynSlice(off, 128)]
                        dst = yr[:, k * 1024:(k + 1) * 1024].rearrange(
                            "p (c t) -> p c t", c=8)
                        self.dma(e, sem_, dst, sap.rearrange("c p t -> p c t"))
            em(eng, gather_y)
            sem_.inc(len(eng_senders) * 4 * 16)
            self.mark(eng, key, sem_, sem_.n)
        mk_gather([0, 1, 2], ld, "o_yr_s", "sync")
        mk_gather([3, 4, 5], lda, "o_yr_a", "scalar")
        mk_gather([6, 7], ldg, "o_yr_g", "gpsimd")

        te0 = te.n

        def w_stream(g):
            g.wait_ge(self.sems["s_ag"].h, d[f"s_ag{NCH-1}"])
            for op_ in range(8):
                for k in range(32):
                    i = op_ * 32 + k
                    if i >= 2:
                        g.wait_ge(te.h, te0 + i - 1)
                    self.dma(g, gldp[i % 2], wt[k % 2][:],
                             H["outT"][k * 128:(k + 1) * 128,
                                       op_ * 256:(op_ + 1) * 256])
                    self.meas[f"o_w{i}"] = self._emitcnt.get(gldp[i % 2].h.num, 0)
        em("gpsimd", w_stream)
        for i in range(8 * 32):
            gldp[i % 2].inc(16)
            d[f"o_w{i}"] = self.actual.get(f"o_w{i}", gldp[i % 2].n)

        ve0 = ve.n

        def mm_out(t):
            t.wait_ge(ld.h, d["o_yr_s"])
            t.wait_ge(lda.h, d["o_yr_a"])
            t.wait_ge(ldg.h, d["o_yr_g"])
            for op_ in range(8):
                if op_ > 0:
                    t.wait_ge(ve.h, ve0 + op_ * 2)
                for k in range(32):
                    i = op_ * 32 + k
                    t.wait_ge(gldp[i % 2].h, d[f"o_w{i}"])
                    last = None
                    for o2 in range(2):
                        for t2 in range(2):
                            last = t.matmul(
                                P[o2][:, t2 * 512:(t2 + 1) * 512],
                                wt[k % 2][:, o2 * 128:(o2 + 1) * 128],
                                yr[:, k * 1024 + t2 * 512:k * 1024 + t2 * 512 + 512],
                                start=(k == 0), stop=(k == 31))
                    last.then_inc(te.h, 1)
        em("tensor", mm_out)
        te.inc(8 * 32)

        def xres_ld(sy):
            sy.wait_ge(self.sems["s_ve"].h, d[f"s_gate{NCH-1}"])
            for ot in range(16):
                if ot >= 2:
                    sy.wait_ge(ve.h, ve0 + ot - 1)
                self.dma(sy, xlp[ot % 2], xr[ot % 2][:],
                         H["x_res"][ot * 128:(ot + 1) * 128, :])
                self.meas[f"o_xr{ot}"] = self._emitcnt.get(xlp[ot % 2].h.num, 0)
        em("sync", xres_ld)
        for ot in range(16):
            xlp[ot % 2].inc(16)
            d[f"o_xr{ot}"] = self.actual.get(f"o_xr{ot}", xlp[ot % 2].n)

        def ev_out(v):
            for op_ in range(8):
                v.wait_ge(te.h, te0 + (op_ + 1) * 32)
                for o2 in range(2):
                    ot = op_ * 2 + o2
                    v.wait_ge(xlp[ot % 2].h, d[f"o_xr{ot}"])
                    v.tensor_tensor(out=self.stile(ot), in0=P[o2][:],
                                    in1=xr[ot % 2][:], op=ALU.add).then_inc(ve.h, 1)
        em("vector", ev_out)
        ve.inc(16)
        d["o_ev"] = ve.n

        self.plan_ln("ln1", dep_sem="o_ve", dep_val=d["o_ev"])

    def plan_ln(self, tag, dep_sem, dep_val, write_bf16=True):
        em, d = self.em, self.d
        te = self.sem(f"{tag}_te")
        ve = self.sem(f"{tag}_ve")
        se = self.sem(f"{tag}_se")
        dep = self.sems[dep_sem]
        sqb = self.rowb.ap().bitcast(BF16)
        R1 = self.P[0][0:1, :]
        R2 = self.P[1][0:1, :]
        RSD = self.P[2]
        MRS = self.P[3]
        se0 = se.n
        te0 = te.n

        def sq_pass(s):
            s.wait_ge(dep.h, dep_val)
            for k in range(16):
                if k >= 2:
                    s.wait_ge(te.h, te0 + k - 1)
                s.activation(out=sqb[:, (k % 2) * 1024:(k % 2) * 1024 + 1024],
                             in_=self.stile(k), func=ACT.Square).then_inc(se.h, 1)
        em("scalar", sq_pass)
        se.inc(16)

        def mm_stats(t):
            t.wait_ge(dep.h, dep_val)
            for k in range(16):
                t.wait_ge(se.h, se0 + k + 1)
                last = None
                for t2 in range(2):
                    t.matmul(R1[:, t2 * 512:(t2 + 1) * 512], self.onesf[:],
                             self.stile(k)[:, t2 * 512:(t2 + 1) * 512],
                             start=(k == 0), stop=(k == 15))
                    last = t.matmul(
                        R2[:, t2 * 512:(t2 + 1) * 512], self.onesb[:],
                        sqb[:, (k % 2) * 1024 + t2 * 512:(k % 2) * 1024 + t2 * 512 + 512],
                        start=(k == 0), stop=(k == 15))
                last.then_inc(te.h, 1)
        em("tensor", mm_stats)
        te.inc(16)
        d[f"{tag}_mm"] = te.n

        def rows(v):
            v.wait_ge(te.h, d[f"{tag}_mm"])
            A = self.rowa[0:1, :]
            Bv = self.rowb[0:1, :]
            i1 = v.tensor_scalar(out=A, in0=R1, scalar1=1.0 / DM, scalar2=None,
                                 op0=ALU.mult)
            self.vedge(v, i1)
            i2 = v.tensor_tensor(out=Bv, in0=A, in1=A, op=ALU.mult)
            self.vedge(v, i2)
            i3 = v.scalar_tensor_tensor(out=Bv, in0=R2, scalar=1.0 / DM, in1=Bv,
                                        op0=ALU.mult, op1=ALU.subtract)
            self.vedge(v, i3)
            v.tensor_scalar(out=Bv, in0=Bv, scalar1=1e-5, scalar2=None,
                            op0=ALU.add).then_inc(ve.h, 1)
        em("vector", rows)
        ve.inc()
        d[f"{tag}_rows"] = ve.n

        def rstd(s):
            s.wait_ge(ve.h, d[f"{tag}_rows"])
            Bv = self.rowb[0:1, :]
            i1 = s.activation(out=Bv, in_=Bv, func=ACT.Ln)
            self.sedge(s, i1)
            s.activation(out=Bv, in_=Bv, func=ACT.Exp, scale=-0.5).then_inc(se.h, 1)
        em("scalar", rstd)
        se.inc()
        d[f"{tag}_rstd"] = se.n

        def mrs(v):
            v.wait_ge(se.h, d[f"{tag}_rstd"])
            v.wait_ge(ve.h, d[f"{tag}_rows"])
            v.tensor_tensor(out=self.rowa[0:1, :], in0=self.rowa[0:1, :],
                            in1=self.rowb[0:1, :], op=ALU.mult).then_inc(ve.h, 1)
        em("vector", mrs)
        ve.inc()
        d[f"{tag}_mrs"] = ve.n

        def bcast(t):
            t.wait_ge(ve.h, d[f"{tag}_mrs"])
            last = None
            for t2 in range(2):
                t.matmul(RSD[:, t2 * 512:(t2 + 1) * 512], self.onesr[:],
                         self.rowb[0:1, t2 * 512:(t2 + 1) * 512], start=True, stop=True)
                last = t.matmul(MRS[:, t2 * 512:(t2 + 1) * 512], self.onesr[:],
                                self.rowa[0:1, t2 * 512:(t2 + 1) * 512],
                                start=True, stop=True)
            last.then_inc(te.h, 1)
        em("tensor", bcast)
        te.inc()
        d[f"{tag}_bc"] = te.n

        def apply(v):
            v.wait_ge(te.h, d[f"{tag}_bc"])
            last = None
            for k in range(16):
                i1 = v.tensor_tensor(out=self.stile(k), in0=self.stile(k), in1=RSD[:],
                                     op=ALU.mult)
                self.vedge(v, i1)
                last = v.tensor_tensor(out=self.stile(k), in0=self.stile(k),
                                       in1=MRS[:], op=ALU.subtract)
                if write_bf16:
                    self.vedge(v, last)
                    last = v.tensor_copy(out=self.sbtile(k), in_=self.stile(k))
            last.then_inc(ve.h, 1)
        em("vector", apply)
        ve.inc()
        d[f"{tag}_done"] = ve.n

    def plan_fake_ln2(self):
        H, em, d = self.H, self.em, self.d
        ld2 = self.sem("x2f_ld")
        fv = self.sem("f_ve")
        d["f_ev7"] = 0

        def loads_f32(sy):
            if ATTN_STEP >= 124:
                sy.nop().then_inc(ld2.h, 16)
                return
            sy.wait_ge(self.sems["c_ld"].h, d["c_ld_done"])
            for k in range(16):
                self.dma(sy, ld2, self.stile(k),
                         H["x_res"][k * 128:(k + 1) * 128, :])
        em("sync", loads_f32)
        ld2.inc(16 if ATTN_STEP >= 124 else 16 * 16)
        self.mark("sync", "x2f_done", ld2, ld2.n)

        gl2 = self.sem("x2g_ld")

        def loads_bf(g):
            for k in range(16):
                self.dma(g, gl2, self.sbtile(k),
                         H["x_res"][k * 128:(k + 1) * 128, :])
        em("gpsimd", loads_bf)
        gl2.inc(16 * 16)
        self.mark("gpsimd", "x2g_done", gl2, gl2.n)

        def bf_wait(g):
            g.wait_ge(gl2.h, d["x2g_done"])
            g.wait_ge(ld2.h, d["x2f_done"])
        em("gpsimd", bf_wait)

        self.sems["ln2_ve"] = ld2
        d["ln2_done"] = d["x2f_done"]

    def plan_bitnet(self):
        H, em, d, P = self.H, self.em, self.d, self.P
        S = self.sb
        wtile = [S(f"f_w{i}", 184320 + i * 8192, 128, 2048, F32) for i in range(2)]
        wq = S("f_wq", 118784, 128, 16 * 2048, BF16)
        iscr = S("f_iscr", 200704, 128, 1024, I32)
        idf = S("identf", 18624, 128, 128, F32)
        mxr = S("f_mxr", 19136, 1, 128, F32)
        cols = S("f_cols", 19680, 128, 8, F32)
        fl = [self.sem("f_ld0"), self.sem("f_ld1")]
        fv = self.sem("f_ve")
        fs = self.sem("f_se")
        ft = self.sem("f_te")
        fc = self.sem("f_cc")
        fy = self.sem("f_st")
        R1 = P[0][0:1, :]
        RSD = P[2]
        ln1d = d["ln1_done"]

        def wload(pass_, base_key):
            def fn(sy, pass_=pass_, base_key=base_key):
                sy.wait_ge(self.sems["ln1_ve"].h, ln1d)
                if pass_ == 2:
                    sy.wait_ge(fv.h, d["f_rcol"])
                for k in range(16):
                    kb = k % 2
                    if k >= 2:
                        sy.wait_ge(fv.h, d[f"f_{base_key}{k-2}"])
                    self.dma(sy, fl[kb], wtile[kb][:],
                             H["l0w_T"][k * 128:(k + 1) * 128, :])
                    self.meas[f"f_wl{pass_}_{k}"] = self._emitcnt.get(fl[kb].h.num, 0)
            return fn
        em("sync", wload(1, "absw"))
        for k in range(16):
            fl[k % 2].inc(16)
            d[f"f_wl1_{k}"] = self.actual.get(f"f_wl1_{k}", fl[k % 2].n)

        def absw(v):
            v.wait_ge(self.sems["ln1_ve"].h, ln1d)
            v.memset(cols[:, 3:4], 0.0)
            i0 = v.tensor_copy(out=idf[:], in_=self.identb[:])
            self.vedge(v, i0)
            for k in range(16):
                v.wait_ge(fl[k % 2].h, d[f"f_wl1_{k}"])
                i1 = v.reduce_sum(out=cols[:, 4:5], in_=wtile[k % 2][:],
                                  axis=mybir.AxisListType.X, apply_absolute_value=True)
                self.vedge(v, i1)
                i2 = v.tensor_tensor(out=cols[:, 3:4], in0=cols[:, 3:4],
                                     in1=cols[:, 4:5], op=ALU.add)
                self.vedge(v, i2)
                v.sem_inc(fv.h, 1)
        em("vector", absw)
        for k in range(16):
            fv.inc()
            d[f"f_absw{k}"] = fv.n

        def mmw(t):
            t.wait_ge(fv.h, d["f_absw15"])
            t.matmul(P[1][0:1, 0:1], self.onesf[:], cols[:, 3:4],
                     start=True, stop=True).then_inc(ft.h, 1)
        em("tensor", mmw)
        ft.inc()
        d["f_mmw"] = ft.n

        def wsc(v):
            v.wait_ge(ft.h, d["f_mmw"])
            i1 = v.tensor_scalar(out=cols[0:1, 5:6], in0=P[1][0:1, 0:1],
                                 scalar1=1.0 / (DM * DM), scalar2=1e-5,
                                 op0=ALU.mult, op1=ALU.max)
            self.vedge(v, i1)
            i2 = v.reciprocal(out=cols[0:1, 6:7], in_=cols[0:1, 5:6])
            self.vedge(v, i2)
            v.sem_inc(fv.h, 1)
        em("vector", wsc)
        fv.inc()
        d["f_wsc"] = fv.n

        def bc_r(t):
            t.wait_ge(fv.h, d["f_wsc"])
            t.matmul(P[1][:, 0:1], self.onesr[:], cols[0:1, 6:7],
                     start=True, stop=True).then_inc(ft.h, 1)
        em("tensor", bc_r)
        ft.inc()
        d["f_bcr"] = ft.n

        def rcol_cp(v):
            v.wait_ge(ft.h, d["f_bcr"])
            v.tensor_copy(out=cols[:, 0:1], in_=P[1][:, 0:1]).then_inc(fv.h, 1)
        em("vector", rcol_cp)
        fv.inc()
        d["f_rcol"] = fv.n

        em("sync", wload(2, "quant"))
        for k in range(16):
            fl[k % 2].inc(16)
            d[f"f_wl2_{k}"] = self.actual.get(f"f_wl2_{k}", fl[k % 2].n)

        LRG, HLF = float(2 ** 30), float(2 ** 29)

        def wquant(v):
            v.wait_ge(fv.h, d["f_rcol"])
            for k in range(16):
                v.wait_ge(fl[k % 2].h, d[f"f_wl2_{k}"])
                i1 = v.tensor_scalar(out=wtile[k % 2][:], in0=wtile[k % 2][:],
                                     scalar1=cols[:, 0:1], scalar2=None,
                                     op0=ALU.mult)
                self.vedge(v, i1)
                for hh in range(2):
                    wh = wtile[k % 2][:, hh * 1024:(hh + 1) * 1024]
                    i2 = v.tensor_scalar(out=self.rowa[:], in0=wh, scalar1=LRG,
                                         scalar2=-HLF, op0=ALU.mult, op1=ALU.add)
                    self.vedge(v, i2)
                    i3 = v.tensor_scalar(out=self.rowa[:], in0=self.rowa[:],
                                         scalar1=0.0, scalar2=1.0,
                                         op0=ALU.max, op1=ALU.min)
                    self.vedge(v, i3)
                    i4 = v.tensor_scalar(out=self.rowb[:], in0=wh, scalar1=-LRG,
                                         scalar2=-HLF, op0=ALU.mult, op1=ALU.add)
                    self.vedge(v, i4)
                    i5 = v.tensor_scalar(out=self.rowb[:], in0=self.rowb[:],
                                         scalar1=0.0, scalar2=1.0,
                                         op0=ALU.max, op1=ALU.min)
                    self.vedge(v, i5)
                    i6 = v.tensor_tensor(
                        out=wq[:, k * 2048 + hh * 1024:k * 2048 + (hh + 1) * 1024],
                        in0=self.rowa[:], in1=self.rowb[:], op=ALU.subtract)
                    self.vedge(v, i6)
                v.nop().then_inc(fv.h, 1)
        em("vector", wquant)
        for k in range(16):
            fv.inc()
            d[f"f_quant{k}"] = fv.n

        sqb = self.rowb.ap().bitcast(BF16)
        fs0, ft0 = fs.n, ft.n

        def sq_pass(s):
            s.wait_ge(self.sems["ln1_ve"].h, ln1d)
            s.wait_ge(fv.h, d["f_quant15"])
            for k in range(16):
                if k >= 2:
                    s.wait_ge(ft.h, ft0 + k - 1)
                s.activation(out=sqb[:, (k % 2) * 1024:(k % 2) * 1024 + 1024],
                             in_=self.stile(k), func=ACT.Square).then_inc(fs.h, 1)
        em("scalar", sq_pass)
        fs.inc(16)

        def mm_stats(t):
            t.wait_ge(fv.h, d["f_quant15"])
            for k in range(16):
                t.wait_ge(fs.h, fs0 + k + 1)
                last = None
                for t2 in range(2):
                    last = t.matmul(
                        R1[:, t2 * 512:(t2 + 1) * 512], self.onesb[:],
                        sqb[:, (k % 2) * 1024 + t2 * 512:(k % 2) * 1024 + t2 * 512 + 512],
                        start=(k == 0), stop=(k == 15))
                last.then_inc(ft.h, 1)
        em("tensor", mm_stats)
        ft.inc(16)
        d["f_mm"] = ft.n

        def rrms(v):
            v.wait_ge(ft.h, d["f_mm"])
            v.tensor_scalar(out=self.rowa[0:1, :], in0=R1, scalar1=1.0 / DM,
                            scalar2=1e-6, op0=ALU.mult,
                            op1=ALU.add).then_inc(fv.h, 1)
        em("vector", rrms)
        fv.inc()
        d["f_var"] = fv.n

        def rrms_s(s):
            s.wait_ge(fv.h, d["f_var"])
            A = self.rowa[0:1, :]
            i1 = s.activation(out=A, in_=A, func=ACT.Ln)
            self.sedge(s, i1)
            s.activation(out=A, in_=A, func=ACT.Exp, scale=-0.5).then_inc(fs.h, 1)
        em("scalar", rrms_s)
        fs.inc()
        d["f_rrms"] = fs.n

        def bc_rr(t):
            t.wait_ge(fs.h, d["f_rrms"])
            last = None
            for t2 in range(2):
                last = t.matmul(RSD[:, t2 * 512:(t2 + 1) * 512], self.onesr[:],
                                self.rowa[0:1, t2 * 512:(t2 + 1) * 512],
                                start=True, stop=True)
            last.then_inc(ft.h, 1)
        em("tensor", bc_rr)
        ft.inc()
        d["f_bcrr"] = ft.n

        def amax(v):
            v.wait_ge(ft.h, d["f_bcrr"])
            v.memset(cols[:, 3:4], 0.0)
            last = None
            for k in range(16):
                i1 = v.scalar_tensor_tensor(out=self.rowb[:], in0=self.stile(k),
                                            scalar=self.gma[:, k:k + 1], in1=RSD[:],
                                            op0=ALU.mult, op1=ALU.mult)
                self.vedge(v, i1)
                i2 = v.reduce_max(out=cols[:, 4:5], in_=self.rowb[:],
                                  axis=mybir.AxisListType.X, apply_absolute_value=True)
                self.vedge(v, i2)
                last = v.tensor_tensor(out=cols[:, 3:4], in0=cols[:, 3:4],
                                       in1=cols[:, 4:5], op=ALU.max)
                self.vedge(v, last)
            v.sem_inc(fv.h, 1)
        em("vector", amax)
        fv.inc()
        d["f_amax"] = fv.n

        def tr_mx(t):
            t.wait_ge(fv.h, d["f_amax"])
            t.transpose(P[3][0:1, 0:128], cols[:, 3:4], idf[:]).then_inc(ft.h, 1)
        em("tensor", tr_mx)
        ft.inc()
        d["f_trmx"] = ft.n

        def mx_cp(v):
            v.wait_ge(ft.h, d["f_trmx"])
            v.tensor_copy(out=mxr[:], in_=P[3][0:1, 0:128]).then_inc(fv.h, 1)
        em("vector", mx_cp)
        fv.inc()
        d["f_mxcp"] = fv.n

        def st_mx(sy):
            sy.wait_ge(fv.h, d["f_mxcp"])
            self.dma(sy, fy, H["mx_in"].unsqueeze(0), mxr[:])
        em("sync", st_mx)
        fy.inc(16)
        self.mark("sync", "f_mxst", fy, fy.n)

        def ar_mx(g):
            g.wait_ge(fy.h, d["f_mxst"])
            g.collective_compute(
                "AllReduce", ALU.max, replica_groups=[list(range(NC8))],
                ins=[H["mx_in"]], outs=[H["mx_out"]],
            ).then_inc(fc.h, 1)
        em("gpsimd", ar_mx)
        fc.inc()
        d["f_armx"] = fc.n

        def rd_mx(sy):
            sy.wait_ge(fc.h, d["f_armx"])
            self.dma(sy, fy, mxr[:], H["mx_out"].unsqueeze(0))
        em("sync", rd_mx)
        fy.inc(16)
        self.mark("sync", "f_mxrd", fy, fy.n)

        def mk_s(v):
            v.wait_ge(fy.h, d["f_mxrd"])
            i1 = v.reduce_max(out=cols[0:1, 6:7], in_=mxr[:],
                              axis=mybir.AxisListType.X)
            self.vedge(v, i1)
            i2 = v.tensor_scalar(out=cols[0:1, 6:7], in0=cols[0:1, 6:7],
                                 scalar1=1e-5, scalar2=None, op0=ALU.max)
            self.vedge(v, i2)
            i3 = v.reciprocal(out=cols[0:1, 7:8], in_=cols[0:1, 6:7])
            self.vedge(v, i3)
            i4 = v.tensor_scalar(out=cols[0:1, 7:8], in0=cols[0:1, 7:8],
                                 scalar1=127.0, scalar2=None, op0=ALU.mult)
            self.vedge(v, i4)
            i5 = v.tensor_tensor(out=cols[0:1, 6:7], in0=cols[0:1, 6:7],
                                 in1=cols[0:1, 5:6], op=ALU.mult)
            self.vedge(v, i5)
            i6 = v.tensor_scalar(out=cols[0:1, 6:7], in0=cols[0:1, 6:7],
                                 scalar1=1.0 / 127.0, scalar2=None, op0=ALU.mult)
            self.vedge(v, i6)
            v.sem_inc(fv.h, 1)
        em("vector", mk_s)
        fv.inc()
        d["f_mks"] = fv.n

        def bc_s(t):
            t.wait_ge(fv.h, d["f_mks"])
            t.matmul(P[3][:, 0:1], self.onesr[:], cols[0:1, 7:8],
                     start=True, stop=True)
            t.matmul(P[3][:, 1:2], self.onesr[:], cols[0:1, 6:7],
                     start=True, stop=True).then_inc(ft.h, 1)
        em("tensor", bc_s)
        ft.inc()
        d["f_bcs"] = ft.n

        def scol_cp(v):
            v.wait_ge(ft.h, d["f_bcs"])
            i1 = v.tensor_copy(out=cols[:, 1:3], in_=P[3][:, 0:2])
            self.vedge(v, i1)
            v.sem_inc(fv.h, 1)
        em("vector", scol_cp)
        fv.inc()
        d["f_scol"] = fv.n

        def aquant(v):
            v.wait_ge(fv.h, d["f_scol"])
            m1 = wtile[0][:, 0:1024]
            m2 = wtile[0][:, 1024:2048]
            for k in range(16):
                i1 = v.scalar_tensor_tensor(out=self.rowb[:], in0=self.stile(k),
                                            scalar=self.gma[:, k:k + 1], in1=RSD[:],
                                            op0=ALU.mult, op1=ALU.mult)
                self.vedge(v, i1)
                i2 = v.tensor_scalar(out=self.rowb[:], in0=self.rowb[:],
                                     scalar1=cols[:, 1:2], scalar2=-128.0,
                                     op0=ALU.mult, op1=ALU.max)
                self.vedge(v, i2)
                i3 = v.tensor_scalar(out=self.rowb[:], in0=self.rowb[:],
                                     scalar1=127.0, scalar2=None, op0=ALU.min)
                self.vedge(v, i3)
                i4 = v.tensor_copy(out=iscr[:], in_=self.rowb[:])
                self.vedge(v, i4)
                i5 = v.tensor_copy(out=self.rowa[:], in_=iscr[:])
                self.vedge(v, i5)
                i6 = v.tensor_tensor(out=self.rowb[:], in0=self.rowb[:],
                                     in1=self.rowa[:], op=ALU.subtract)
                self.vedge(v, i6)
                i7 = v.tensor_scalar(out=m1, in0=self.rowb[:], scalar1=LRG,
                                     scalar2=-HLF, op0=ALU.mult, op1=ALU.add)
                self.vedge(v, i7)
                i8 = v.tensor_scalar(out=m1, in0=m1, scalar1=0.0, scalar2=1.0,
                                     op0=ALU.max, op1=ALU.min)
                self.vedge(v, i8)
                i9 = v.tensor_scalar(out=m2, in0=self.rowb[:], scalar1=-LRG,
                                     scalar2=-HLF, op0=ALU.mult, op1=ALU.add)
                self.vedge(v, i9)
                i10 = v.tensor_scalar(out=m2, in0=m2, scalar1=0.0, scalar2=1.0,
                                      op0=ALU.max, op1=ALU.min)
                self.vedge(v, i10)
                i11 = v.tensor_tensor(out=self.rowa[:], in0=self.rowa[:],
                                      in1=m1, op=ALU.add)
                self.vedge(v, i11)
                v.tensor_tensor(out=self.sbtile(k), in0=self.rowa[:],
                                in1=m2, op=ALU.subtract).then_inc(fv.h, 1)
        em("vector", aquant)
        for k in range(16):
            fv.inc()
            d[f"f_kq{k}"] = fv.n

        def mm_ffn(t):
            t.wait_ge(fv.h, d["f_kq15"])
            for op_ in range(8):
                if op_ > 0:
                    t.wait_ge(fv.h, d[f"f_ev{op_-1}"])
                last = None
                for k in range(16):
                    for o2 in range(2):
                        oc = op_ * 2 + o2
                        for t2 in range(2):
                            last = t.matmul(
                                P[o2][:, t2 * 512:(t2 + 1) * 512],
                                wq[:, k * 2048 + oc * 128:k * 2048 + (oc + 1) * 128],
                                self.sbtile(k)[:, t2 * 512:(t2 + 1) * 512],
                                start=(k == 0), stop=(k == 15))
                last.then_inc(ft.h, 1)
        em("tensor", mm_ffn)
        ft.inc(8)
        d["f_mmf"] = ft.n

        def ev_ffn(v):
            for op_ in range(8):
                v.wait_ge(ft.h, d["f_mmf"] - 8 + op_ + 1)
                last = None
                for o2 in range(2):
                    ot = op_ * 2 + o2
                    last = v.scalar_tensor_tensor(
                        out=self.stile(ot), in0=P[o2][:], scalar=cols[:, 2:3],
                        in1=self.stile(ot), op0=ALU.mult, op1=ALU.add)
                    self.vedge(v, last)
                v.sem_inc(fv.h, 1)
        em("vector", ev_ffn)
        for op_ in range(8):
            fv.inc()
            d[f"f_ev{op_}"] = fv.n

        if DEBUG_BITNET:
            dbs = self.sem("dbg_st")

            def dbg_dump(s):
                s.wait_ge(fv.h, d["f_ev7"])
                self.dma(s, dbs, H["dbg"][:, 0:5], cols[:, 0:5])
                self.dma(s, dbs, H["dbg"][0:1, 5:8], cols[0:1, 5:8])
                self.dma(s, dbs, H["dbg"][0:1, 8:1032], self.rowa[0:1, :])
                self.dma(s, dbs, H["dbg"][0:1, 1032:1160], mxr[:])
                self.dma(s, dbs, H["dbgb"][:, 0:32768], wq[:])
                for k in range(16):
                    self.dma(s, dbs,
                             H["dbgb"][:, 32768 + k * 1024:32768 + (k + 1) * 1024],
                             self.sbtile(k))
            em("scalar", dbg_dump)
            dbs.inc(16 * 21)
            self.mark("scalar", "dbg_done", dbs, dbs.n)

            def dbg_wait(s):
                s.wait_ge(dbs.h, d["dbg_done"])
            em("scalar", dbg_wait)
            self.sems["ln2_ve"] = fv
            d["ln2_done"] = d["f_ev7"]
        else:
            self.plan_ln("ln2", dep_sem="f_ve", dep_val=d["f_ev7"])

    def plan_attn(self):
        H, em, d, P = self.H, self.em, self.d, self.P
        S = self.sb
        nqp = S("a_nqp", 20480, 128, 12 * 1024, BF16)
        nkvp = S("a_nkvp", 45056, 128, 4 * 1024, BF16)
        krst = S("a_krst", 53248, 64, 2048, BF16)
        cqw = S("a_cqw", 118784, 128, 16 * 1536, BF16)
        ckvw = S("a_ckvw", 167936, 128, 16 * 512, BF16)
        krw = S("a_krw", 184320, 128, 2 * 16 * 64, BF16)
        dqnw = S("a_dqnw", 188416, 128, 12 * 256, BF16)
        dqrw = S("a_dqrw", 194560, 128, 12 * 128, BF16)
        dqr2w = S("a_dqr2w", 197632, 128, 12 * 128, BF16)
        dknw = S("a_dknw", 200704, 128, 4 * 256, BF16)
        dvww = S("a_dvww", 202752, 128, 4 * 256, BF16)
        probs = S("a_probs", 204800, 128, 2 * 512, BF16)
        qn_all = S("a_qn", 20480, 128, 2 * 8192, BF16)
        kn_all = S("a_kn", 53248, 128, 2 * 8192, BF16)
        qr_all = S("a_qr", 86016, 128, 8192, BF16)
        kr_all = S("a_kr", 102400, 128, 8192, BF16)
        vt_all = S("a_vt", 118784, 128, 2 * 8192, BF16)
        nqc = S("a_nqcb", 151552, 128, 12 * 1024, BF16)
        kvc = S("a_kvcb", 176128, 128, 4 * 1024, BF16)
        krc = S("a_krcb", 184320, 128, 2048, BF16)
        aoin = S("a_aoin", 151552, 128, 16 * 1024, BF16)
        wpw = [S(f"a_wpw{i}", 184320 + i * 1024, 128, 512, BF16) for i in range(2)]
        xr2 = [S(f"a_xr2{i}", 186368 + i * 4096, 128, 1024, F32) for i in range(2)]

        aspl = self.sem("a_spl")
        agl = self.sem("a_gl")
        ate = self.sem("a_te")
        ave = self.sem("a_ve")
        ase = self.sem("a_se")
        ast = self.sem("a_stg")
        acc = self.sem("a_cc")
        akl = self.sem("a_kl")
        anl = self.sem("a_nl")
        fst = [self.sem("a_fst0"), self.sem("a_fst1")]
        agr = [self.sem("a_gr0"), self.sem("a_gr1"), self.sem("a_gr2")]
        awp = [self.sem("a_wp0"), self.sem("a_wp1")]
        axl = [self.sem("a_xl0"), self.sem("a_xl1")]

        sqb2 = self.rowb.ap().bitcast(BF16)
        x2v = H["x2_sp"].rearrange("(k p t) -> k p t", k=16, p=128)
        KVE = (RKV + 2 * DR) * TMINE

        def half(ps, t2):
            return ps[:, t2 * 512:(t2 + 1) * 512]

        def a_spill(sy):
            sy.wait_ge(self.sems["ln2_ve"].h, d["ln2_done"])
            if ATTN_STEP >= 119:
                return
            for k in range(16):
                self.dma(sy, aspl, x2v[k], self.stile(k))
        em("sync", a_spill)
        if ATTN_STEP < 119:
            aspl.inc(16 * 16)
        self.mark("sync", "a_spl", aspl, aspl.n)

        def a_wloads(g):
            g.wait_ge(self.sems["f_ve"].h, d["f_ev7"])
            for k in range(16):
                self.dma(g, agl, cqw[:, k * 1536:(k + 1) * 1536],
                         H["cq"][k * 128:(k + 1) * 128, :])
            if ATTN_STEP >= 120:
                return
            for k in range(16):
                self.dma(g, agl, ckvw[:, k * 512:(k + 1) * 512],
                         H["ckv"][k * 128:(k + 1) * 128, :])
            for k in range(16):
                self.dma(g, agl, krw[:, k * 64:(k + 1) * 64],
                         H["krope_w"][k * 128:(k + 1) * 128, :])
                self.dma(g, agl, krw[:, 1024 + k * 64:1024 + (k + 1) * 64],
                         H["krope_w2"][k * 128:(k + 1) * 128, :])
            for rb in range(12):
                self.dma(g, agl, dqnw[:, rb * 256:(rb + 1) * 256],
                         H["dq_n"][rb * 128:(rb + 1) * 128, :])
                self.dma(g, agl, dqrw[:, rb * 128:(rb + 1) * 128],
                         H["dq_r"][rb * 128:(rb + 1) * 128, :])
                self.dma(g, agl, dqr2w[:, rb * 128:(rb + 1) * 128],
                         H["dq_r2"][rb * 128:(rb + 1) * 128, :])
            for rb in range(4):
                self.dma(g, agl, dknw[:, rb * 256:(rb + 1) * 256],
                         H["dk_n"][rb * 128:(rb + 1) * 128, :])
                self.dma(g, agl, dvww[:, rb * 256:(rb + 1) * 256],
                         H["dv_w"][rb * 128:(rb + 1) * 128, :])
        em("gpsimd", a_wloads)
        agl.inc((16 if ATTN_STEP >= 120 else (16 * 4 + 12 * 3 + 8)) * 16)
        self.mark("gpsimd", "a_wl", agl, agl.n)

        if ATTN_STEP == 11:
            self.sems["ln3_ve"] = agl
            d["ln3_done"] = d["a_wl"]
            return

        def a_cqmm(t):
            t.wait_ge(agl.h, d["a_wl"])
            t.wait_ge(self.sems["ln2_ve"].h, d["ln2_done"])
            for rb in range(ATTN_RB):
                for t2 in range(2):
                    if rb >= 1:
                        t.wait_ge(ase.h, d[f"aq_ev{rb-1}_{t2}"])
                    last = None
                    for k in range(16):
                        last = t.matmul(
                            half(P[0], t2),
                            cqw[:, k * 1536 + rb * 128:k * 1536 + (rb + 1) * 128],
                            half(self.sbtile(k), t2), start=(k == 0), stop=(k == 15))
                    last.then_inc(ate.h, 1)
                last = None
                for t2 in range(2):
                    t.wait_ge(ase.h, d[f"aq_sq{rb}_{t2}"])
                    last = t.matmul(
                        P[1][0:1, t2 * 512:(t2 + 1) * 512], self.onesb[:],
                        sqb2[:, (rb % 2) * 1024 + t2 * 512:(rb % 2) * 1024 + t2 * 512 + 512],
                        start=(rb == 0), stop=(rb == ATTN_RB - 1))
                last.then_inc(ate.h, 1)
        em("tensor", a_cqmm)
        for rb in range(ATTN_RB):
            for t2 in range(2):
                ate.inc()
                d[f"aq_mm{rb}_{t2}"] = ate.n
            ate.inc()
            d[f"aq_R{rb}"] = ate.n

        def a_cqsq(s):
            s.wait_ge(aspl.h, d["a_spl"])
            for rb in range(ATTN_RB):
                for t2 in range(2):
                    s.wait_ge(ate.h, d[f"aq_mm{rb}_{t2}"])
                    if rb >= 2:
                        s.wait_ge(ate.h, d[f"aq_R{rb-2}"])
                    s.activation(
                        out=sqb2[:, (rb % 2) * 1024 + t2 * 512:(rb % 2) * 1024 + t2 * 512 + 512],
                        in_=half(P[0], t2), func=ACT.Square).then_inc(ase.h, 1)
                    s.activation(
                        out=nqp[:, rb * 1024 + t2 * 512:rb * 1024 + t2 * 512 + 512],
                        in_=half(P[0], t2), func=ACT.Copy).then_inc(ase.h, 1)
        em("scalar", a_cqsq)
        for rb in range(ATTN_RB):
            for t2 in range(2):
                ase.inc()
                d[f"aq_sq{rb}_{t2}"] = ase.n
                ase.inc()
                d[f"aq_ev{rb}_{t2}"] = ase.n

        if ATTN_STEP == 111:
            self.sems["ln3_ve"] = ase
            d["ln3_done"] = d[f"aq_ev{ATTN_RB-1}_1"]
            return

        def a_nqrows(v):
            v.wait_ge(ate.h, d["aq_R11"])
            v.tensor_scalar(out=self.rowa[0:1, :], in0=P[1][0:1, :],
                            scalar1=1.0 / RQ, scalar2=1e-5, op0=ALU.mult,
                            op1=ALU.add).then_inc(ave.h, 1)
        em("vector", a_nqrows)
        ave.inc()
        d["aq_row"] = ave.n

        def a_nqrstd(s):
            s.wait_ge(ave.h, d["aq_row"])
            A = self.rowa[0:1, :]
            i1 = s.activation(out=A, in_=A, func=ACT.Ln)
            self.sedge(s, i1)
            s.activation(out=A, in_=A, func=ACT.Exp, scale=-0.5).then_inc(ase.h, 1)
        em("scalar", a_nqrstd)
        ase.inc()
        d["aq_rstd"] = ase.n

        def a_nqbc(t):
            t.wait_ge(ase.h, d["aq_rstd"])
            last = None
            for t2 in range(2):
                last = t.matmul(half(P[2], t2), self.onesr[:],
                                self.rowa[0:1, t2 * 512:(t2 + 1) * 512],
                                start=True, stop=True)
            last.then_inc(ate.h, 1)
        em("tensor", a_nqbc)
        ate.inc()
        d["aq_bc"] = ate.n

        if ATTN_STEP == 115:
            self.sems["ln3_ve"] = ate
            d["ln3_done"] = d["aq_bc"]
            return

        def a_nqscale(v):
            v.wait_ge(ate.h, d["aq_bc"])
            last = None
            for rb in range(12):
                last = v.scalar_tensor_tensor(
                    out=nqp[:, rb * 1024:(rb + 1) * 1024],
                    in0=nqp[:, rb * 1024:(rb + 1) * 1024],
                    scalar=self.qgam[:, rb:rb + 1], in1=P[2][:, :],
                    op0=ALU.mult, op1=ALU.mult)
            last.then_inc(ave.h, 1)
        em("vector", a_nqscale)
        ave.inc()
        d["aq_scaled"] = ave.n

        if ATTN_STEP == 12:
            self.sems["ln3_ve"] = ave
            d["ln3_done"] = d["aq_scaled"]
            return

        def a_kvmm(t):
            for rb in range(4):
                for t2 in range(2):
                    if rb >= 1:
                        t.wait_ge(ase.h, d[f"ak_ev{rb-1}_{t2}"])
                    last = None
                    for k in range(16):
                        last = t.matmul(
                            half(P[3], t2),
                            ckvw[:, k * 512 + rb * 128:k * 512 + (rb + 1) * 128],
                            half(self.sbtile(k), t2), start=(k == 0), stop=(k == 15))
                    last.then_inc(ate.h, 1)
                if rb == 0:
                    t.wait_ge(ave.h, d["aq_row"])
                last = None
                for t2 in range(2):
                    t.wait_ge(ase.h, d[f"ak_sq{rb}_{t2}"])
                    last = t.matmul(
                        P[1][0:1, t2 * 512:(t2 + 1) * 512], self.onesb[:],
                        sqb2[:, (rb % 2) * 1024 + t2 * 512:(rb % 2) * 1024 + t2 * 512 + 512],
                        start=(rb == 0), stop=(rb == 3))
                last.then_inc(ate.h, 1)
        em("tensor", a_kvmm)
        for rb in range(4):
            for t2 in range(2):
                ate.inc()
                d[f"ak_mm{rb}_{t2}"] = ate.n
            ate.inc()
            d[f"ak_R{rb}"] = ate.n

        def a_kvsq(s):
            for rb in range(4):
                for t2 in range(2):
                    s.wait_ge(ate.h, d[f"ak_mm{rb}_{t2}"])
                    if rb >= 2:
                        s.wait_ge(ate.h, d[f"ak_R{rb-2}"])
                    s.activation(
                        out=sqb2[:, (rb % 2) * 1024 + t2 * 512:(rb % 2) * 1024 + t2 * 512 + 512],
                        in_=half(P[3], t2), func=ACT.Square).then_inc(ase.h, 1)
                    s.activation(
                        out=nkvp[:, rb * 1024 + t2 * 512:rb * 1024 + t2 * 512 + 512],
                        in_=half(P[3], t2), func=ACT.Copy).then_inc(ase.h, 1)
        em("scalar", a_kvsq)
        for rb in range(4):
            for t2 in range(2):
                ase.inc()
                d[f"ak_sq{rb}_{t2}"] = ase.n
                ase.inc()
                d[f"ak_ev{rb}_{t2}"] = ase.n

        def a_kvrows(v):
            v.wait_ge(ate.h, d["ak_R3"])
            v.tensor_scalar(out=self.rowa[0:1, :], in0=P[1][0:1, :],
                            scalar1=1.0 / RKV, scalar2=1e-5, op0=ALU.mult,
                            op1=ALU.add).then_inc(ave.h, 1)
        em("vector", a_kvrows)
        ave.inc()
        d["ak_row"] = ave.n

        def a_kvrstd(s):
            s.wait_ge(ave.h, d["ak_row"])
            A = self.rowa[0:1, :]
            i1 = s.activation(out=A, in_=A, func=ACT.Ln)
            self.sedge(s, i1)
            s.activation(out=A, in_=A, func=ACT.Exp, scale=-0.5).then_inc(ase.h, 1)
        em("scalar", a_kvrstd)
        ase.inc()
        d["ak_rstd"] = ase.n

        def a_kvbc(t):
            t.wait_ge(ase.h, d["ak_rstd"])
            t.wait_ge(ave.h, d["aq_scaled"])
            last = None
            for t2 in range(2):
                last = t.matmul(half(P[2], t2), self.onesr[:],
                                self.rowa[0:1, t2 * 512:(t2 + 1) * 512],
                                start=True, stop=True)
            last.then_inc(ate.h, 1)
        em("tensor", a_kvbc)
        ate.inc()
        d["ak_bc"] = ate.n

        def a_kvscale(v):
            v.wait_ge(ate.h, d["ak_bc"])
            last = None
            for rb in range(4):
                last = v.scalar_tensor_tensor(
                    out=nkvp[:, rb * 1024:(rb + 1) * 1024],
                    in0=nkvp[:, rb * 1024:(rb + 1) * 1024],
                    scalar=self.kvgam[:, rb:rb + 1], in1=P[2][:, :],
                    op0=ALU.mult, op1=ALU.mult)
            last.then_inc(ave.h, 1)
        em("vector", a_kvscale)
        ave.inc()
        d["ak_scaled"] = ave.n

        if ATTN_STEP == 13:
            self.sems["ln3_ve"] = ave
            d["ln3_done"] = d["ak_scaled"]
            return

        def a_krmm(t):
            t.wait_ge(ase.h, d["aq_ev11_1"])
            for t2 in range(2):
                last = None
                for k in range(16):
                    last = t.matmul(P[0][0:64, t2 * 512:(t2 + 1) * 512],
                                    krw[:, k * 64:(k + 1) * 64],
                                    half(self.sbtile(k), t2),
                                    start=(k == 0), stop=(k == 15))
                last.then_inc(ate.h, 1)
            t.wait_ge(ase.h, d["ak_ev3_1"])
            for t2 in range(2):
                last = None
                for k in range(16):
                    last = t.matmul(P[3][0:64, t2 * 512:(t2 + 1) * 512],
                                    krw[:, 1024 + k * 64:1024 + (k + 1) * 64],
                                    half(self.sbtile(k), t2),
                                    start=(k == 0), stop=(k == 15))
                last.then_inc(ate.h, 1)
        em("tensor", a_krmm)
        for t2 in range(2):
            ate.inc()
            d[f"akr_raw{t2}"] = ate.n
        for t2 in range(2):
            ate.inc()
            d[f"akr_sw{t2}"] = ate.n

        def a_krev(s):
            last = None
            for t2 in range(2):
                s.wait_ge(ate.h, d[f"akr_raw{t2}"])
                last = s.activation(out=krst[:, t2 * 512:(t2 + 1) * 512],
                                    in_=P[0][0:64, t2 * 512:(t2 + 1) * 512],
                                    func=ACT.Copy)
            for t2 in range(2):
                s.wait_ge(ate.h, d[f"akr_sw{t2}"])
                last = s.activation(out=krst[:, 1024 + t2 * 512:1024 + (t2 + 1) * 512],
                                    in_=P[3][0:64, t2 * 512:(t2 + 1) * 512],
                                    func=ACT.Copy)
            last.then_inc(ase.h, 1)
        em("scalar", a_krev)
        ase.inc()
        d["akr_ev"] = ase.n

        def a_stage(sy):
            sy.wait_ge(ave.h, d["aq_scaled"])
            sy.wait_ge(ave.h, d["ak_scaled"])
            sy.wait_ge(ase.h, d["akr_ev"])
            self.dma(sy, ast, H["nq_in"].rearrange("(a p t) -> p a t", a=12, p=128),
                     nqp[:].rearrange("p (a t) -> p a t", a=12))
            self.dma(sy, ast,
                     H["kv_in"][0:RKV * TMINE].rearrange("(a p t) -> p a t", a=4, p=128),
                     nkvp[:].rearrange("p (a t) -> p a t", a=4))
            self.dma(sy, ast,
                     H["kv_in"][RKV * TMINE:].rearrange("(a p t) -> p a t", a=2, p=64),
                     krst[:].rearrange("p (a t) -> p a t", a=2))
        em("sync", a_stage)
        ast.inc(3 * 16)
        self.mark("sync", "a_stg", ast, ast.n)

        if ATTN_STEP == 1:
            self.sems["ln3_ve"] = ast
            d["ln3_done"] = d["a_stg"]
            return

        def a_ags(g):
            g.wait_ge(ast.h, d["a_stg"])
            g.collective_compute(
                "AllGather", ALU.bypass, replica_groups=[list(range(NC8))],
                ins=[H["kv_in"]], outs=[H["kv_out"]]).then_inc(acc.h, 1)
            g.collective_compute(
                "AllGather", ALU.bypass, replica_groups=[list(range(NC8))],
                ins=[H["nq_in"]], outs=[H["nq_out"]]).then_inc(acc.h, 1)
        em("gpsimd", a_ags)
        acc.inc()
        d["a_kvag"] = acc.n
        acc.inc()
        d["a_nqag"] = acc.n

        if ATTN_STEP == 2:
            self.sems["ln3_ve"] = acc
            d["ln3_done"] = d["a_nqag"]
            return

        kn5 = kn_all[:].rearrange("p (h j c2 i) -> p h j c2 i", h=2, j=8, c2=8)
        qn5 = qn_all[:].rearrange("p (h j c2 i) -> p h j c2 i", h=2, j=8, c2=8)
        kr5 = kr_all[:].rearrange("p (j c2 i) -> p j c2 i", j=8, c2=8)
        qr5 = qr_all[:].rearrange("p (j c2 i) -> p j c2 i", j=8, c2=8)

        def a_kvload(sy):
            sy.wait_ge(acc.h, d["a_kvag"])
            for c in range(8):
                if c >= 1:
                    sy.wait_ge(ate.h, d[f"ab_kv{c-1}"])
                    sy.wait_ge(ave.h, d[f"ab_kr{c-1}"])
                src = H["kv_out"][c * KVE:(c + 1) * KVE]
                self.dma(sy, akl, kvc[:].rearrange("p (a t) -> p a t", a=4),
                         src[0:RKV * TMINE].rearrange("(a p t) -> p a t", a=4, p=128))
                self.dma(sy, akl, krc[0:64, :].rearrange("p (a t) -> p a t", a=2),
                         src[RKV * TMINE:].rearrange("(a p t) -> p a t", a=2, p=64))
                self.dma(sy, akl, krc[64:128, :].rearrange("p (a t) -> p a t", a=2),
                         src[RKV * TMINE:].rearrange("(a p t) -> p a t", a=2, p=64))
                self.meas[f"a_kvc{c}"] = self._emitcnt.get(akl.h.num, 0)
        em("sync", a_kvload)
        for c in range(8):
            akl.inc(3 * 16)
            d[f"a_kvc{c}"] = self.actual.get(f"a_kvc{c}", akl.n)

        def a_kvbuild(t):
            for c in range(8):
                t.wait_ge(akl.h, d[f"a_kvc{c}"])
                if c == 0:
                    t.wait_ge(ase.h, d["akr_ev"])
                    t.wait_ge(ave.h, d["ak_scaled"])
                else:
                    t.wait_ge(ase.h, d[f"ab_knev{c-1}"])
                    t.wait_ge(ase.h, d[f"ab_vtev{c-1}"])
                for hh in range(2):
                    for t2 in range(2):
                        bank = half([P[0], P[2]][hh], t2)
                        last = None
                        for rb in range(4):
                            last = t.matmul(
                                bank, dknw[:, rb * 256 + hh * 128:rb * 256 + (hh + 1) * 128],
                                kvc[:, rb * 1024 + t2 * 512:rb * 1024 + t2 * 512 + 512],
                                start=(rb == 0), stop=(rb == 3))
                        last.then_inc(ate.h, 1)
                for hh in range(2):
                    for j in range(8):
                        idx = hh * 8 + j
                        if idx >= 2:
                            t.wait_ge(ase.h, d[f"ab_vtev{c}_{idx-2}"])
                        elif c > 0:
                            t.wait_ge(ase.h, d[f"ab_vtev{c-1}_{idx+14}"])
                        bank = P[1][:, (idx % 2) * 512:(idx % 2) * 512 + 128]
                        last = None
                        for rb in range(4):
                            last = t.matmul(
                                bank, kvc[:, rb * 1024 + j * 128:rb * 1024 + (j + 1) * 128],
                                dvww[:, rb * 256 + hh * 128:rb * 256 + (hh + 1) * 128],
                                start=(rb == 0), stop=(rb == 3))
                        last.then_inc(ate.h, 1)
        em("tensor", a_kvbuild)
        for c in range(8):
            for hh in range(2):
                for t2 in range(2):
                    ate.inc()
                    d[f"ab_kn{c}_{hh}_{t2}"] = ate.n
            for idx in range(16):
                ate.inc()
                d[f"ab_vt{c}_{idx}"] = ate.n
            d[f"ab_kv{c}"] = ate.n

        def a_kvbev(s):
            for c in range(8):
                for hh in range(2):
                    for t2 in range(2):
                        s.wait_ge(ate.h, d[f"ab_kn{c}_{hh}_{t2}"])
                        s.activation(out=kn5[:, hh, 4 * t2:4 * t2 + 4, c, :],
                                     in_=half([P[0], P[2]][hh], t2),
                                     func=ACT.Copy).then_inc(ase.h, 1)
                for hh in range(2):
                    for j in range(8):
                        idx = hh * 8 + j
                        s.wait_ge(ate.h, d[f"ab_vt{c}_{idx}"])
                        b = c + 8 * j
                        s.activation(
                            out=vt_all[:, hh * 8192 + b * 128:hh * 8192 + (b + 1) * 128],
                            in_=P[1][:, (idx % 2) * 512:(idx % 2) * 512 + 128],
                            func=ACT.Copy, scale=V_SCALE).then_inc(ase.h, 1)
        em("scalar", a_kvbev)
        for c in range(8):
            for i in range(4):
                ase.inc()
            d[f"ab_knev{c}"] = ase.n
            for idx in range(16):
                ase.inc()
                d[f"ab_vtev{c}_{idx}"] = ase.n
            d[f"ab_vtev{c}"] = ase.n

        def a_krrot(v):
            for c in range(8):
                for t2 in range(2):
                    v.wait_ge(akl.h, d[f"a_kvc{c}"])
                    i1 = v.tensor_scalar(
                        out=self.rowa[:, 0:512],
                        in0=krc[:, 1024 + t2 * 512:1024 + (t2 + 1) * 512],
                        scalar1=self.nsinc[:, 0:1], scalar2=None, op0=ALU.mult)
                    self.vedge(v, i1)
                    i2 = v.scalar_tensor_tensor(
                        out=kr5[:, 4 * t2:4 * t2 + 4, c, :],
                        in0=krc[:, t2 * 512:(t2 + 1) * 512],
                        scalar=self.cosc[:, 0:1], in1=self.rowa[:, 0:512],
                        op0=ALU.mult, op1=ALU.add)
                    self.vedge(v, i2)
                v.sem_inc(ave.h, 1)
        em("vector", a_krrot)
        for c in range(8):
            ave.inc()
            d[f"ab_kr{c}"] = ave.n

        def a_nqload(sy):
            sy.wait_ge(acc.h, d["a_nqag"])
            for c in range(8):
                if c >= 1:
                    sy.wait_ge(ate.h, d[f"ab_q{c-1}"])
                self.dma(sy, anl, nqc[:].rearrange("p (a t) -> p a t", a=12),
                         H["nq_out"][c * RQ * TMINE:(c + 1) * RQ * TMINE]
                         .rearrange("(a p t) -> p a t", a=12, p=128))
                self.meas[f"a_nqc{c}"] = self._emitcnt.get(anl.h.num, 0)
        em("sync", a_nqload)
        for c in range(8):
            anl.inc(16)
            d[f"a_nqc{c}"] = self.actual.get(f"a_nqc{c}", anl.n)

        def a_qbuild(t):
            for c in range(8):
                t.wait_ge(anl.h, d[f"a_nqc{c}"])
                if c == 0:
                    t.wait_ge(ase.h, d["ab_knev7"])
                    t.wait_ge(ase.h, d["ab_vtev7"])
                    t.wait_ge(ave.h, d["ab_kr7"])
                else:
                    t.wait_ge(ase.h, d[f"ab_qnev{c-1}"])
                    t.wait_ge(ave.h, d[f"ab_qr{c-1}"])
                for hh in range(2):
                    for t2 in range(2):
                        bank = half([P[0], P[2]][hh], t2)
                        last = None
                        for rb in range(12):
                            last = t.matmul(
                                bank, dqnw[:, rb * 256 + hh * 128:rb * 256 + (hh + 1) * 128],
                                nqc[:, rb * 1024 + t2 * 512:rb * 1024 + t2 * 512 + 512],
                                start=(rb == 0), stop=(rb == 11))
                        last.then_inc(ate.h, 1)
                for t2 in range(2):
                    last = None
                    for rb in range(12):
                        last = t.matmul(
                            half(P[3], t2), dqrw[:, rb * 128:(rb + 1) * 128],
                            nqc[:, rb * 1024 + t2 * 512:rb * 1024 + t2 * 512 + 512],
                            start=(rb == 0), stop=(rb == 11))
                    last.then_inc(ate.h, 1)
                for t2 in range(2):
                    last = None
                    for rb in range(12):
                        last = t.matmul(
                            half(P[1], t2), dqr2w[:, rb * 128:(rb + 1) * 128],
                            nqc[:, rb * 1024 + t2 * 512:rb * 1024 + t2 * 512 + 512],
                            start=(rb == 0), stop=(rb == 11))
                    last.then_inc(ate.h, 1)
        em("tensor", a_qbuild)
        for c in range(8):
            for hh in range(2):
                for t2 in range(2):
                    ate.inc()
                    d[f"ab_qn{c}_{hh}_{t2}"] = ate.n
            for t2 in range(2):
                ate.inc()
                d[f"ab_qraw{c}_{t2}"] = ate.n
            for t2 in range(2):
                ate.inc()
                d[f"ab_qsw{c}_{t2}"] = ate.n
            d[f"ab_q{c}"] = ate.n

        def a_qbev(s):
            for c in range(8):
                for hh in range(2):
                    for t2 in range(2):
                        s.wait_ge(ate.h, d[f"ab_qn{c}_{hh}_{t2}"])
                        s.activation(out=qn5[:, hh, 4 * t2:4 * t2 + 4, c, :],
                                     in_=half([P[0], P[2]][hh], t2),
                                     func=ACT.Copy, scale=ATTN_SCALE).then_inc(ase.h, 1)
        em("scalar", a_qbev)
        for c in range(8):
            for i in range(4):
                ase.inc()
            d[f"ab_qnev{c}"] = ase.n

        def a_qrot(v):
            for c in range(8):
                for t2 in range(2):
                    v.wait_ge(ate.h, d[f"ab_qraw{c}_{t2}"])
                    v.wait_ge(ate.h, d[f"ab_qsw{c}_{t2}"])
                    i1 = v.tensor_scalar(out=self.rowa[:, 0:512], in0=half(P[1], t2),
                                         scalar1=self.nscs[:, 0:1], scalar2=None,
                                         op0=ALU.mult)
                    self.vedge(v, i1)
                    i2 = v.scalar_tensor_tensor(
                        out=qr5[:, 4 * t2:4 * t2 + 4, c, :], in0=half(P[3], t2),
                        scalar=self.cscs[:, 0:1], in1=self.rowa[:, 0:512],
                        op0=ALU.mult, op1=ALU.add)
                    self.vedge(v, i2)
                v.sem_inc(ave.h, 1)
        em("vector", a_qrot)
        for c in range(8):
            ave.inc()
            d[f"ab_qr{c}"] = ave.n

        if ATTN_STEP == 3:
            self.sems["ln3_ve"] = ave
            d["ln3_done"] = d["ab_qr7"]
            return

        order = [(qc, hh) for qc in range(16) for hh in range(2)]
        s_val, pv_val, bc_val, exp_val = {}, {}, {}, {}
        dc_val, cp_val, rc_val, dv_val = {}, {}, {}, {}
        ten, sen, ven = ate.n, ase.n, ave.n
        for (qc, hh) in order:
            nkt = 4 * qc + 4
            ten += 1
            s_val[(qc, hh, 0)] = ten
            for kt in range(nkt):
                if kt + 1 < nkt:
                    ten += 1
                    s_val[(qc, hh, kt + 1)] = ten
                ten += 1
                pv_val[(qc, hh, kt)] = ten
            ten += 1
            bc_val[(qc, hh)] = ten
            for kt in range(nkt):
                sen += 1
                exp_val[(qc, hh, kt)] = sen
            sen += 1
            dc_val[(qc, hh)] = sen
            sen += 1
            cp_val[(qc, hh)] = sen
            ven += 1
            rc_val[(qc, hh)] = ven
            ven += 1
            dv_val[(qc, hh)] = ven
        aob = self.rowa.ap().bitcast(BF16)

        def a_flash_te(t):
            t.wait_ge(ase.h, d["ab_qnev7"])
            t.wait_ge(ave.h, d["ab_qr7"])
            prev = None
            for (qc, hh) in order:
                nkt = 4 * qc + 4
                qn_s = qn_all[:, hh * 8192 + qc * 512:hh * 8192 + (qc + 1) * 512]
                qr_s = qr_all[hh * 64:(hh + 1) * 64, qc * 512:(qc + 1) * 512]
                if prev is not None:
                    t.wait_ge(ave.h, dv_val[prev])
                    t.wait_ge(ase.h, exp_val[prev + (4 * prev[0] + 3,)])

                def emit_S(kt):
                    sb = half(P[0], kt % 2)
                    t.matmul(sb, kn_all[:, hh * 8192 + kt * 128:hh * 8192 + (kt + 1) * 128],
                             qn_s, start=True, stop=False)
                    if kt >= 4 * qc:
                        o = kt - 4 * qc
                        t.matmul(sb, self.identb[:], self.masks[:, o * 512:(o + 1) * 512],
                                 start=False, stop=False)
                    t.matmul(sb, kr_all[hh * 64:(hh + 1) * 64, kt * 128:(kt + 1) * 128],
                             qr_s, start=False, stop=True).then_inc(ate.h, 1)
                emit_S(0)
                for kt in range(nkt):
                    if kt + 1 < nkt:
                        if kt >= 1:
                            t.wait_ge(ase.h, exp_val[(qc, hh, kt - 1)])
                        emit_S(kt + 1)
                    t.wait_ge(ase.h, exp_val[(qc, hh, kt)])
                    pr = probs[:, (kt % 2) * 512:(kt % 2) * 512 + 512]
                    t.matmul(P[1][:, 0:512],
                             vt_all[:, hh * 8192 + kt * 128:hh * 8192 + (kt + 1) * 128],
                             pr, start=(kt == 0), stop=(kt == nkt - 1))
                    t.matmul(P[2][0:1, 0:512], self.onesb[:], pr,
                             start=(kt == 0), stop=(kt == nkt - 1)).then_inc(ate.h, 1)
                t.wait_ge(ave.h, rc_val[(qc, hh)])
                t.matmul(P[2][:, 512:1024], self.onesr[:], self.rowb[0:1, 0:512],
                         start=True, stop=True).then_inc(ate.h, 1)
                prev = (qc, hh)
        em("tensor", a_flash_te)
        ate.inc(ten - ate.n)

        def a_flash_se(s):
            for i, (qc, hh) in enumerate(order):
                nkt = 4 * qc + 4
                for kt in range(nkt):
                    s.wait_ge(ate.h, s_val[(qc, hh, kt)])
                    if kt >= 2:
                        s.wait_ge(ate.h, pv_val[(qc, hh, kt - 2)])
                    s.activation(out=probs[:, (kt % 2) * 512:(kt % 2) * 512 + 512],
                                 in_=half(P[0], kt % 2),
                                 func=ACT.Exp).then_inc(ase.h, 1)
                s.wait_ge(ate.h, pv_val[(qc, hh, nkt - 1)])
                s.activation(out=self.rowb[0:1, 0:512], in_=P[2][0:1, 0:512],
                             func=ACT.Copy).then_inc(ase.h, 1)
                s.wait_ge(ate.h, bc_val[(qc, hh)])
                if i >= 1:
                    s.wait_ge(ave.h, dv_val[order[i - 1]])
                s.activation(out=self.rowb[:, 512:1024], in_=P[2][:, 512:1024],
                             func=ACT.Copy).then_inc(ase.h, 1)
        em("scalar", a_flash_se)
        ase.inc(sen - ase.n)

        def a_flash_ve(v):
            v.wait_ge(ave.h, d["ab_qr7"])
            for i, (qc, hh) in enumerate(order):
                v.wait_ge(ase.h, dc_val[(qc, hh)])
                v.reciprocal(out=self.rowb[0:1, 0:512],
                             in_=self.rowb[0:1, 0:512]).then_inc(ave.h, 1)
                v.wait_ge(ase.h, cp_val[(qc, hh)])
                if i >= 2:
                    pq, ph = order[i - 2]
                    v.wait_ge(fst[i % 2].h, d[f"a_ao{pq}_{ph}"])
                v.tensor_tensor(out=aob[:, (i % 2) * 512:(i % 2) * 512 + 512],
                                in0=P[1][:, 0:512], in1=self.rowb[:, 512:1024],
                                op=ALU.mult).then_inc(ave.h, 1)
        em("vector", a_flash_ve)
        ave.inc(ven - ave.n)

        aov = H["ao_in"].rearrange("(h p o j i) -> h p o j i", h=2, p=128, o=8,
                                   j=8, i=128)

        def a_flash_st(sy):
            for i, (qc, hh) in enumerate(order):
                sy.wait_ge(ave.h, dv_val[(qc, hh)])
                o0 = (4 * qc) % 8
                jc = qc // 2
                self.dma(sy, fst[i % 2], aov[hh][:, o0:o0 + 4, jc, :],
                         aob[:, (i % 2) * 512:(i % 2) * 512 + 512]
                         .rearrange("p (a i) -> p a i", a=4))
                self.meas[f"a_ao{qc}_{hh}"] = self._emitcnt.get(fst[i % 2].h.num, 0)
        em("sync", a_flash_st)
        for i, (qc, hh) in enumerate(order):
            fst[i % 2].inc(16)
            d[f"a_ao{qc}_{hh}"] = self.actual.get(f"a_ao{qc}_{hh}", fst[i % 2].n)

        if DEBUG_ATTN:
            dbsem = self.sem("a_dbg")

            def a_dbg(sy):
                sy.wait_ge(fst[0].h, d["a_ao15_0"])
                sy.wait_ge(fst[1].h, d["a_ao15_1"])
                self.dma(sy, dbsem, H["dbga"][:, 0:16384], qn_all[:])
                self.dma(sy, dbsem, H["dbga"][:, 16384:32768], kn_all[:])
                self.dma(sy, dbsem, H["dbga"][:, 32768:40960], qr_all[:])
                self.dma(sy, dbsem, H["dbga"][:, 40960:49152], kr_all[:])
                self.dma(sy, dbsem, H["dbga"][:, 49152:65536], vt_all[:])
            em("sync", a_dbg)
            dbsem.inc(5 * 16)
            self.mark("sync", "a_dbgd", dbsem, dbsem.n)

            def a_dbgw(sy):
                sy.wait_ge(dbsem.h, d["a_dbgd"])
            em("sync", a_dbgw)

        if ATTN_STEP == 4:
            self.sems["ln3_ve"] = ave
            d["ln3_done"] = dv_val[order[-1]]
            return

        def a_aoag(g):
            g.wait_ge(fst[0].h, d["a_ao15_0"])
            g.wait_ge(fst[1].h, d["a_ao15_1"])
            g.collective_compute(
                "AllGather", ALU.bypass, replica_groups=[list(range(NC8))],
                ins=[H["ao_in"]], outs=[H["ao_out"]]).then_inc(acc.h, 1)
        em("gpsimd", a_aoag)
        acc.inc()
        d["a_aoag"] = acc.n

        ao6 = H["ao_out"].rearrange("(c h p o j i) -> c h p o j i", c=8, h=2,
                                    p=128, o=8, j=8, i=128)
        pairs = [(c, hh) for c in range(8) for hh in range(2)]

        def mk_aog(plist, sem_, key, eng):
            def g(e, plist=plist):
                e.wait_ge(acc.h, d["a_aoag"])
                me_r = e.value_load(self.coffs[0:1, 0:1])
                for (c, hh) in plist:
                    src = ao6[c, hh][:, bass.DynSlice(me_r, 1), :, :]
                    dst = aoin[:, (2 * c + hh) * 1024:(2 * c + hh + 1) * 1024] \
                        .rearrange("p (j i) -> p j i", j=8)
                    self.dma(e, sem_, dst, src.rearrange("p a j i -> p (a j) i"))
            em(eng, g)
            sem_.inc(len(plist) * 16)
            self.mark(eng, key, sem_, sem_.n)
        mk_aog(pairs[0:6], agr[0], "a_gr0", "sync")
        mk_aog(pairs[6:11], agr[1], "a_gr1", "scalar")
        mk_aog(pairs[11:16], agr[2], "a_gr2", "gpsimd")

        if ATTN_STEP == 5:
            self.sems["ln3_ve"] = agr[2]
            d["ln3_done"] = d["a_gr2"]
            return

        def a_wpl(g):
            g.wait_ge(acc.h, d["a_aoag"])
            for dbg in range(4):
                for h in range(16):
                    i = dbg * 16 + h
                    if i >= 2:
                        g.wait_ge(ate.h, d[f"ap_mm{(i-2)//16}_{(i-2)%16}"])
                    self.dma(g, awp[i % 2], wpw[i % 2][:],
                             H["wproj"][h * 128:(h + 1) * 128,
                                        dbg * 512:(dbg + 1) * 512])
                    self.meas[f"a_wp{i}"] = self._emitcnt.get(awp[i % 2].h.num, 0)
        em("gpsimd", a_wpl)
        for i in range(64):
            awp[i % 2].inc(16)
            d[f"a_wp{i}"] = self.actual.get(f"a_wp{i}", awp[i % 2].n)

        def a_x2l(sy):
            sy.wait_ge(acc.h, d["a_aoag"])
            for db in range(16):
                if db >= 2:
                    sy.wait_ge(ave.h, d[f"ap_evt{db-2}"])
                self.dma(sy, axl[db % 2], xr2[db % 2][:], x2v[db])
                self.meas[f"a_x2l{db}"] = self._emitcnt.get(axl[db % 2].h.num, 0)
        em("sync", a_x2l)
        for db in range(16):
            axl[db % 2].inc(16)
            d[f"a_x2l{db}"] = self.actual.get(f"a_x2l{db}", axl[db % 2].n)

        def a_proj(t):
            t.wait_ge(agr[0].h, d["a_gr0"])
            t.wait_ge(agr[1].h, d["a_gr1"])
            t.wait_ge(agr[2].h, d["a_gr2"])
            t.wait_ge(ave.h, dv_val[order[-1]])
            for dbg in range(4):
                if dbg >= 1:
                    t.wait_ge(ave.h, d[f"ap_ev{dbg-1}"])
                for h in range(16):
                    i = dbg * 16 + h
                    t.wait_ge(awp[i % 2].h, d[f"a_wp{i}"])
                    last = None
                    for db2 in range(4):
                        for t2 in range(2):
                            last = t.matmul(
                                half(P[db2], t2), wpw[i % 2][:, db2 * 128:(db2 + 1) * 128],
                                aoin[:, h * 1024 + t2 * 512:h * 1024 + t2 * 512 + 512],
                                start=(h == 0), stop=(h == 15))
                    last.then_inc(ate.h, 1)
        em("tensor", a_proj)
        for dbg in range(4):
            for h in range(16):
                ate.inc()
                d[f"ap_mm{dbg}_{h}"] = ate.n

        def a_projev(v):
            for dbg in range(4):
                v.wait_ge(ate.h, d[f"ap_mm{dbg}_15"])
                for db2 in range(4):
                    db = dbg * 4 + db2
                    v.wait_ge(axl[db % 2].h, d[f"a_x2l{db}"])
                    i1 = v.tensor_tensor(out=self.stile(db)[:, 0:512],
                                         in0=half(P[db2], 0), in1=xr2[db % 2][:, 0:512],
                                         op=ALU.add)
                    self.vedge(v, i1)
                    v.tensor_tensor(out=self.stile(db)[:, 512:1024],
                                    in0=half(P[db2], 1), in1=xr2[db % 2][:, 512:1024],
                                    op=ALU.add).then_inc(ave.h, 1)
        em("vector", a_projev)
        for dbg in range(4):
            for db2 in range(4):
                ave.inc()
                d[f"ap_evt{dbg*4+db2}"] = ave.n
            d[f"ap_ev{dbg}"] = ave.n

        self.plan_ln("ln3", dep_sem="a_ve", dep_val=d["ap_ev3"])

    def plan_moe(self):
        H, em, d, P = self.H, self.em, self.d, self.P
        S = self.sb
        maw = S("e_maw", 118784, 128, 16 * 512, BF16)
        mbw = S("e_mbw", 135168, 128, 4 * 2048, BF16)
        mwrw = S("e_mwrw", 151552, 128, 16 * 8, BF16)
        brcol = S("e_brc", 151872, 8, 1, F32)
        lgs = S("e_lgs", 152064, 8, 1024, F32)
        lgt = S("e_lgt", 156160, 128, 64, F32)
        selb = S("e_selb", 156416, 8, 1024, BF16)
        mrw = S("e_mrw", 158464, 128, 4 * 1024, BF16)
        hsb = S("e_hsb", 166656, 128, 4 * 1024, BF16)
        smat = S("e_smat", 174848, 8, 512, BF16)
        idf = S("identf", 18624, 128, 128, F32)
        egl = self.sem("e_gl")
        esl = self.sem("e_sl")
        ete = self.sem("e_te")
        ese = self.sem("e_se")
        eve = self.sem("e_ve")
        LRG = float(2 ** 30)

        def half(ps, t2):
            return ps[:, t2 * 512:(t2 + 1) * 512]

        dep = self.sems["ln3_ve"]
        lnd = d["ln3_done"]

        def e_wloads(g):
            g.wait_ge(dep.h, lnd)
            for k in range(16):
                self.dma(g, egl, mwrw[:, k * 8:(k + 1) * 8],
                         H["moe_wr"][k * 128:(k + 1) * 128, :])
            for k in range(16):
                self.dma(g, egl, maw[:, k * 512:(k + 1) * 512],
                         H["moe_a"][k * 128:(k + 1) * 128, :])
            for rt in range(4):
                self.dma(g, egl, mbw[:, rt * 2048:(rt + 1) * 2048],
                         H["moe_b"][rt * 128:(rt + 1) * 128, :])
        em("gpsimd", e_wloads)
        egl.inc(36 * 16)
        self.mark("gpsimd", "e_wl", egl, egl.n)

        def e_brl(sy):
            sy.wait_ge(dep.h, lnd)
            self.dma(sy, esl, brcol[:], H["moe_br"])
            self.dma(sy, esl, smat[:], H["selmat"])
        em("sync", e_brl)
        esl.inc(32)
        self.mark("sync", "e_brl", esl, esl.n)

        def e_mm_lg(t):
            t.wait_ge(egl.h, d["e_wl"])
            t.wait_ge(dep.h, lnd)
            for t2 in range(2):
                last = None
                for k in range(16):
                    last = t.matmul(P[0][0:8, t2 * 512:(t2 + 1) * 512],
                                    mwrw[:, k * 8:(k + 1) * 8],
                                    half(self.sbtile(k), t2),
                                    start=(k == 0), stop=(k == 15))
                last.then_inc(ete.h, 1)
        em("tensor", e_mm_lg)
        ete.inc(2)
        d["e_lg"] = ete.n

        def e_lgev(s):
            s.wait_ge(ete.h, d["e_lg"])
            last = None
            for t2 in range(2):
                last = s.activation(out=lgs[0:8, t2 * 512:(t2 + 1) * 512],
                                    in_=P[0][0:8, t2 * 512:(t2 + 1) * 512],
                                    func=ACT.Copy)
            last.then_inc(ese.h, 1)
        em("scalar", e_lgev)
        ese.inc()
        d["e_lgev"] = ese.n

        def e_lgbias(v):
            v.wait_ge(ese.h, d["e_lgev"])
            v.wait_ge(esl.h, d["e_brl"])
            v.tensor_scalar(out=lgs[0:8, :], in0=lgs[0:8, :],
                            scalar1=brcol[0:8, 0:1], scalar2=None,
                            op0=ALU.add).then_inc(eve.h, 1)
        em("vector", e_lgbias)
        eve.inc()
        d["e_lgb"] = eve.n

        def e_tr1(t):
            t.wait_ge(eve.h, d["e_lgb"])
            last = None
            for g in range(8):
                bank = P[g // 2][:, (g % 2) * 512:(g % 2) * 512 + 8]
                last = t.transpose(bank, lgs[0:8, g * 128:(g + 1) * 128],
                                   idf[0:8, 0:8])
            last.then_inc(ete.h, 1)
        em("tensor", e_tr1)
        ete.inc()
        d["e_tr1"] = ete.n

        def e_lgtev(s):
            s.wait_ge(ete.h, d["e_tr1"])
            for g in range(8):
                i1 = s.activation(out=lgt[:, g * 8:(g + 1) * 8],
                                  in_=P[g // 2][:, (g % 2) * 512:(g % 2) * 512 + 8],
                                  func=ACT.Copy)
                self.sedge(s, i1)
            s.activation(out=self.rowa[:, 0:64], in_=lgt[:, 0:64],
                         func=ACT.Exp).then_inc(ese.h, 1)
        em("scalar", e_lgtev)
        ese.inc()
        d["e_exp"] = ese.n

        def e_smax(v):
            v.wait_ge(ese.h, d["e_exp"])
            R = self.rowa
            ex3 = R[:, 0:64].rearrange("p (g e) -> p g e", e=8)
            i = v.reduce_sum(out=R[:, 64:72].rearrange("p (g e) -> p g e", e=1),
                             in_=ex3, axis=mybir.AxisListType.X)
            self.vedge(v, i)
            i = v.reciprocal(out=R[:, 72:80], in_=R[:, 64:72])
            self.vedge(v, i)
            i = v.reduce_max(out=R[:, 80:88].rearrange("p (g e) -> p g e", e=1),
                             in_=ex3, axis=mybir.AxisListType.X)
            self.vedge(v, i)
            df3 = R[:, 128:192].rearrange("p (g e) -> p g e", e=8)
            for e in range(8):
                i = v.tensor_tensor(out=df3[:, :, e], in0=ex3[:, :, e],
                                    in1=R[:, 80:88], op=ALU.subtract)
                self.vedge(v, i)
            i = v.tensor_scalar(out=R[:, 128:192], in0=R[:, 128:192], scalar1=LRG,
                                scalar2=1.0, op0=ALU.mult, op1=ALU.add)
            self.vedge(v, i)
            i = v.tensor_scalar(out=R[:, 128:192], in0=R[:, 128:192], scalar1=0.0,
                                scalar2=1.0, op0=ALU.max, op1=ALU.min)
            self.vedge(v, i)
            m3 = R[:, 128:192].rearrange("p (g e) -> p g e", e=8)
            i = v.reduce_sum(out=R[:, 88:96].rearrange("p (g e) -> p g e", e=1),
                             in_=m3, axis=mybir.AxisListType.X)
            self.vedge(v, i)
            i = v.reciprocal(out=R[:, 96:104], in_=R[:, 88:96])
            self.vedge(v, i)
            i = v.tensor_tensor(out=R[:, 104:112], in0=R[:, 80:88],
                                in1=R[:, 72:80], op=ALU.mult)
            self.vedge(v, i)
            i = v.tensor_tensor(out=R[:, 104:112], in0=R[:, 104:112],
                                in1=R[:, 96:104], op=ALU.mult)
            self.vedge(v, i)
            i = v.tensor_scalar(out=R[:, 104:112], in0=R[:, 104:112],
                                scalar1=1.0 / 64.0, scalar2=None, op0=ALU.mult)
            self.vedge(v, i)
            mw3 = R[:, 192:256].rearrange("p (g e) -> p g e", e=8)
            last = None
            for e in range(8):
                last = v.tensor_tensor(out=mw3[:, :, e], in0=m3[:, :, e],
                                       in1=R[:, 104:112], op=ALU.mult)
                self.vedge(v, last)
            v.sem_inc(eve.h, 1)
        em("vector", e_smax)
        eve.inc()
        d["e_smax"] = eve.n

        def e_tr2(t):
            t.wait_ge(eve.h, d["e_smax"])
            last = None
            for g in range(8):
                bank = P[g // 2][0:8, (g % 2) * 512:(g % 2) * 512 + 128]
                last = t.transpose(bank, self.rowa[:, 192 + g * 8:192 + (g + 1) * 8],
                                   idf[:])
            last.then_inc(ete.h, 1)
        em("tensor", e_tr2)
        ete.inc()
        d["e_tr2"] = ete.n

        def e_selev(s):
            s.wait_ge(ete.h, d["e_tr2"])
            last = None
            for g in range(8):
                last = s.activation(
                    out=selb[0:8, g * 128:(g + 1) * 128],
                    in_=P[g // 2][0:8, (g % 2) * 512:(g % 2) * 512 + 128],
                    func=ACT.Copy)
            last.then_inc(ese.h, 1)
        em("scalar", e_selev)
        ese.inc()
        d["e_sel"] = ese.n

        def e_mrep(t):
            t.wait_ge(ese.h, d["e_sel"])
            last = None
            for rt in range(4):
                for t2 in range(2):
                    last = t.matmul(half(P[rt], t2),
                                    smat[0:8, rt * 128:(rt + 1) * 128],
                                    selb[0:8, t2 * 512:(t2 + 1) * 512],
                                    start=True, stop=True)
            last.then_inc(ete.h, 1)
        em("tensor", e_mrep)
        ete.inc()
        d["e_mrep"] = ete.n

        def e_mrev(s):
            s.wait_ge(ete.h, d["e_mrep"])
            last = None
            for rt in range(4):
                for t2 in range(2):
                    last = s.activation(
                        out=mrw[:, rt * 1024 + t2 * 512:rt * 1024 + t2 * 512 + 512],
                        in_=half(P[rt], t2), func=ACT.Copy)
            last.then_inc(ese.h, 1)
        em("scalar", e_mrev)
        ese.inc()
        d["e_mrev"] = ese.n

        def e_hmm(t):
            t.wait_ge(ese.h, d["e_mrev"])
            last = None
            for rt in range(4):
                for t2 in range(2):
                    for k in range(16):
                        last = t.matmul(
                            half(P[rt], t2),
                            maw[:, k * 512 + rt * 128:k * 512 + (rt + 1) * 128],
                            half(self.sbtile(k), t2),
                            start=(k == 0), stop=(k == 15))
            last.then_inc(ete.h, 1)
        em("tensor", e_hmm)
        ete.inc()
        d["e_hmm"] = ete.n

        def e_hev(v):
            v.wait_ge(ete.h, d["e_hmm"])
            last = None
            for rt in range(4):
                for t2 in range(2):
                    last = v.tensor_tensor(
                        out=hsb[:, rt * 1024 + t2 * 512:rt * 1024 + t2 * 512 + 512],
                        in0=half(P[rt], t2),
                        in1=mrw[:, rt * 1024 + t2 * 512:rt * 1024 + t2 * 512 + 512],
                        op=ALU.mult)
                    self.vedge(v, last)
            v.sem_inc(eve.h, 1)
        em("vector", e_hev)
        eve.inc()
        d["e_hev"] = eve.n

        def e_ymm(t):
            t.wait_ge(eve.h, d["e_hev"])
            for ob in range(16):
                if ob >= 4:
                    t.wait_ge(eve.h, d[f"e_yev{ob-4}"])
                last = None
                for t2 in range(2):
                    for rt in range(4):
                        last = t.matmul(
                            half(P[ob % 4], t2),
                            mbw[:, rt * 2048 + ob * 128:rt * 2048 + (ob + 1) * 128],
                            hsb[:, rt * 1024 + t2 * 512:rt * 1024 + t2 * 512 + 512],
                            start=(rt == 0), stop=(rt == 3))
                last.then_inc(ete.h, 1)
        em("tensor", e_ymm)
        for ob in range(16):
            ete.inc()
            d[f"e_ymm{ob}"] = ete.n

        def e_yev(v):
            for ob in range(16):
                v.wait_ge(ete.h, d[f"e_ymm{ob}"])
                last = None
                for t2 in range(2):
                    last = v.tensor_tensor(
                        out=self.stile(ob)[:, t2 * 512:(t2 + 1) * 512],
                        in0=half(P[ob % 4], t2),
                        in1=self.stile(ob)[:, t2 * 512:(t2 + 1) * 512],
                        op=ALU.add)
                    self.vedge(v, last)
                v.sem_inc(eve.h, 1)
        em("vector", e_yev)
        for ob in range(16):
            eve.inc()
            d[f"e_yev{ob}"] = eve.n

        self.plan_ln("ln4", dep_sem="e_ve", dep_val=d["e_yev15"])

    def plan_out(self):
        H, em, d = self.H, self.em, self.d
        tag = ("ln3" if PHASE_LIMIT == -3 else
               {1: "ln1", 2: "ln2", 3: "ln3", 4: "ln4"}[PHASE_LIMIT])
        so = self.sem("out_st")
        base = so.n
        dv = d[f"{tag}_done"]

        def store(sy):
            sy.wait_ge(self.sems[f"{tag}_ve"].h, dv)
            for k in range(1 if ATTN_STEP >= 122 else 16):
                self.dma(sy, so, H["out"][k * 128:(k + 1) * 128, :], self.stile(k))
            self.meas["out_done"] = self._emitcnt.get(so.h.num, 0)
            sy.wait_ge(so.h, d["out_done"])
        em("sync", store)
        so.inc((16 if ATTN_STEP >= 122 else 16 * 16))
        d["out_done"] = self.actual.get("out_done", so.n)


def _prep_inputs(inputs):
    x = np.ascontiguousarray(inputs["L0_attn_hidden_states"][0])
    xT = np.ascontiguousarray(x.T)
    inwT = np.ascontiguousarray(inputs["L0_attn_in_proj_w"].T)
    xpT = np.ascontiguousarray(inputs["L0_attn_x_proj_w"].T)
    dtT = np.ascontiguousarray(inputs["L0_attn_dt_proj_w"].T)
    outT = np.ascontiguousarray(inputs["L0_attn_out_proj_w"].T)
    l0wT = np.ascontiguousarray(inputs["L0_ffn_weight"].T)
    deint = np.concatenate([np.arange(0, DR, 2), np.arange(1, DR, 2)])
    deint2 = np.concatenate([np.arange(1, DR, 2), np.arange(0, DR, 2)])
    kropeW = np.ascontiguousarray(inputs["L1_attn_w_k_rope"][:, deint])
    kropeW2 = np.ascontiguousarray(inputs["L1_attn_w_k_rope"][:, deint2])
    moe_a = np.ascontiguousarray(inputs["L1_ffn_A"].transpose(2, 0, 1).reshape(DM, 512))
    moe_b = np.ascontiguousarray(inputs["L1_ffn_B"].transpose(0, 2, 1).reshape(512, DM))
    import ml_dtypes
    identb = np.eye(128, dtype=np.float32).astype(ml_dtypes.bfloat16)
    selmat = np.zeros((8, 512), np.float32)
    for rt in range(4):
        for p in range(128):
            selmat[2 * rt + p // 64, rt * 128 + p] = 1.0
    selmat_b = selmat.astype(ml_dtypes.bfloat16)
    onescol = np.ones((128, 1), ml_dtypes.bfloat16)
    masks = np.zeros((128, 4 * QT), np.float32)
    ii = np.arange(128)[:, None]
    jj = np.arange(QT)[None, :]
    for o in range(4):
        masks[:, o * QT:(o + 1) * QT] = np.where(o * 128 + ii <= jj, 0.0, -60000.0)
    masksb = masks.astype(ml_dtypes.bfloat16)
    cos = inputs["L1_attn_freqs_cos"]
    sin = inputs["L1_attn_freqs_sin"]

    per_core = []
    for me in range(NC8):
        blocks = [me + 8 * k for k in range(8)]
        tok = np.concatenate([np.arange(b * TBLK, (b + 1) * TBLK) for b in blocks])
        h0, h1 = 2 * me, 2 * me + 1
        c2 = np.zeros((HPC * DR, 1), np.float32)
        s2 = np.zeros((HPC * DR, 1), np.float32)
        for hh, h in enumerate((h0, h1)):
            c2[hh * DR:hh * DR + 32, 0] = cos[h]
            c2[hh * DR + 32:hh * DR + 64, 0] = cos[h]
            s2[hh * DR:hh * DR + 32, 0] = sin[h]
            s2[hh * DR + 32:hh * DR + 64, 0] = sin[h]
        im = dict(
            xT=xT,
            x_res=np.ascontiguousarray(xT[:, tok]),
            inpT=np.ascontiguousarray(
                np.concatenate([inwT[:, me * DIC:(me + 1) * DIC],
                                inwT[:, DI + me * DIC:DI + (me + 1) * DIC]], axis=1)),
            conv_w=np.ascontiguousarray(inputs["L0_attn_conv_w"][me * DIC:(me + 1) * DIC]),
            conv_b=np.ascontiguousarray(
                inputs["L0_attn_conv_b"][me * DIC:(me + 1) * DIC][:, None]),
            xpT=np.ascontiguousarray(xpT[me * DIC:(me + 1) * DIC]),
            dtT=np.ascontiguousarray(dtT[:, me * DIC:(me + 1) * DIC]),
            dt_b=np.ascontiguousarray(
                inputs["L0_attn_dt_proj_b"][me * DIC:(me + 1) * DIC][:, None]),
            Alog=np.ascontiguousarray(inputs["L0_attn_A_log"][me * DIC:(me + 1) * DIC]),
            Dvec=np.ascontiguousarray(inputs["L0_attn_D"][me * DIC:(me + 1) * DIC][:, None]),
            outT=outT,
            l0w_T=l0wT,
            l0_gamma=np.ascontiguousarray(inputs["L0_ffn_norm_weight"][:, None]),
            cq=inputs["L1_attn_w_cq"],
            q_gamma=np.ascontiguousarray(inputs["L1_attn_q_norm_w"][:, None]),
            dq_n=np.ascontiguousarray(inputs["L1_attn_w_dq_nope"][:, h0 * DN:(h1 + 1) * DN]),
            dq_r=np.ascontiguousarray(
                np.concatenate([inputs["L1_attn_w_dq_rope"][:, h * DR:(h + 1) * DR][:, deint]
                                for h in (h0, h1)], axis=1)),
            dq_r2=np.ascontiguousarray(
                np.concatenate([inputs["L1_attn_w_dq_rope"][:, h * DR:(h + 1) * DR][:, deint2]
                                for h in (h0, h1)], axis=1)),
            ckv=inputs["L1_attn_w_ckv"],
            kv_gamma=np.ascontiguousarray(inputs["L1_attn_kv_norm_w"][:, None]),
            dk_n=np.ascontiguousarray(inputs["L1_attn_w_dk_nope"][:, h0 * DN:(h1 + 1) * DN]),
            dv_w=np.ascontiguousarray(inputs["L1_attn_w_dv"][:, h0 * DV:(h1 + 1) * DV]),
            krope_w=kropeW,
            krope_w2=kropeW2,
            cos2h=c2, sin2h=s2,
            sgnrope=np.tile(np.concatenate([np.full(32, -1.0, np.float32),
                                            np.full(32, 1.0, np.float32)]), 2)[:, None],
            wproj=inputs["L1_attn_w_proj"],
            moe_wr=np.ascontiguousarray(inputs["L1_ffn_Wr"].T),
            moe_br=np.ascontiguousarray(inputs["L1_ffn_br"][:, None]),
            moe_a=moe_a, moe_b=moe_b,
            selmat=selmat_b,
            identb=identb, onescol=onescol, masks=masksb,
            coff=np.full((8, 1), me, np.int32),
        )
        per_core.append(im)
    return per_core


def kernel(**inputs):
    nc = build_program2()
    per_core = _prep_inputs(inputs)
    res = run_bass_kernel_spmd(nc, per_core, core_ids=list(range(NC8)))
    out = np.zeros((T, DM), np.float32)
    for me in range(NC8):
        o = res.results[me]["out"]
        for bi in range(8):
            b = me + 8 * bi
            out[b * TBLK:(b + 1) * TBLK, :] = o[:, bi * TBLK:(bi + 1) * TBLK].T
    return out[None, :, :]
